# version 32
# baseline (speedup 1.0000x reference)
import contextlib
import numpy as np
import ml_dtypes
import concourse.bass as bass
import concourse.mybir as mybir
from concourse.bass_utils import run_bass_kernel_spmd

F32 = mybir.dt.float32
BF16 = mybir.dt.bfloat16
AF = mybir.ActivationFunctionType
OP = mybir.AluOpType
AX = mybir.AxisListType

P = 128
C = 512
NMACRO = 16
SEQ_MACROS = 8
TOK = C * NMACRO
D = 1024
DI = 2048
NDC = D // P
NIC = DI // P
NEXP = 6
NKC = 56
EPS = 1e-5

ENGINES = ("tensor", "vector", "scalar", "sync", "gpsimd")
NSETS = 18
DEBUG_TAPS = False
BF_CACHE = True
TRUNC = 5


class EM:

    def __init__(self, nc, es):
        self.nc = nc
        self.es = es
        self.ops = {k: [] for k in ENGINES}
        self.semh = {}
        self.cnt = {}
        self.lastw = {}

    def sem(self, name):
        self.semh[name] = self.es.enter_context(self.nc.semaphore(name))
        self.cnt[name] = 0

    def wait(self, eng, ev):
        if ev is None:
            return
        nm, val = ev
        key = (eng, nm)
        if self.lastw.get(key, 0) >= val:
            return
        self.lastw[key] = val
        h = self.semh[nm]
        self.ops[eng].append(lambda E, h=h, v=val: E.wait_ge(h, v))

    def op(self, eng, fn, waits=(), inc=None, inc_by=1):
        for ev in waits:
            self.wait(eng, ev)
        if inc is not None:
            self.cnt[inc] += inc_by
            ev_out = (inc, self.cnt[inc])
            h = self.semh[inc]
            self.ops[eng].append(lambda E, fn=fn, h=h, b=inc_by: fn(E).then_inc(h, b))
            return ev_out
        self.ops[eng].append(lambda E, fn=fn: fn(E))
        return None

    def ev(self, name):
        return (name, self.cnt[name])


def build_nc():
    nc = bass.Bass("TRN2", target_bir_lowering=False, detect_race_conditions=False)

    dr = {}

    def din(name, shape, dt=F32):
        dr[name] = nc.dram_tensor(name, list(shape), dt, kind="ExternalInput").ap()
        return dr[name]

    x_hbm = din("x", [D, TOK])
    out_hbm = nc.dram_tensor("out", [D, TOK], F32, kind="ExternalOutput").ap()
    if DEBUG_TAPS:
        dbg_y = nc.dram_tensor("dbg_y", [NIC * P, TOK], F32, kind="ExternalOutput").ap()
        dbg_r = nc.dram_tensor("dbg_r", [D, TOK], F32, kind="ExternalOutput").ap()

    din("winTa", [D, DI])
    din("winTb", [D, DI])
    din("woutT", [DI, D])
    din("ew1T", [NEXP * D, D])
    din("ew3T", [NEXP * D, D])
    din("ew2T", [NEXP * D, D])
    din("fc1T", [D, D])
    din("fc2T", [D, D])
    din("fc3T", [D, D])
    din("wxcT", [DI, 16])
    din("blogT", [16, DI])
    din("gwT", [D, 8])
    din("gb", [1, 8])
    din("convw", [P, NIC * 4])
    din("convb", [P, NIC])
    din("lng", [P, NDC])
    din("lnb", [P, NDC])
    din("identB", [P, P], BF16)
    WSLAB_COLS = 13 * 16384 + 8192
    dr["wslab"] = nc.dram_tensor("wslab", [P, WSLAB_COLS], BF16, kind="Internal").ap()
    din("riota", [P, 8])
    din("onehot", [NEXP, NEXP * P], BF16)

    es = contextlib.ExitStack()

    def sb(name, shape, dt):
        return es.enter_context(nc.sbuf_tensor(name, list(shape), dt))[:]

    x2_sb = sb("x2_sb", [P, 2 * NDC * C], F32)
    x_sb = x2_sb[:, 0:NDC * C]
    xbf = sb("xbf", [P, NDC * C], BF16)
    xh = sb("xh", [P, NDC * C], BF16)
    uni = sb("uni", [P, NKC * C], BF16)
    slot = [sb(f"slot{i}", [P, 16384], BF16) for i in range(2)]
    xin2 = sb("xin2", [P, 2 * (C + 3)], BF16)
    cvt = sb("cvt", [P, 2 * C], BF16)
    h2 = sb("h2", [P, 2 * C], F32)
    dec = sb("dec", [P, C], F32)
    g2b = sb("g2b", [P, 2 * C], BF16)
    r2rot = sb("r2rot", [P, 2 * C], BF16)
    ssil = sb("ssil", [P, 2 * C], BF16)
    hist = sb("hist", [P, NIC * 3], BF16)
    hinit = sb("hinit", [P, NIC], F32)
    identB = sb("identB_sb", [P, P], BF16)
    riota = sb("riota_sb", [P, 8], F32)
    gwTf = sb("gwTf", [P, NDC * 8], F32)
    gwTb = sb("gwTb", [P, NDC * 8], BF16)
    wxcf = sb("wxcf", [P, NIC * 16], F32)
    wxcb = sb("wxcb", [P, NIC * 16], BF16)
    B_T = sb("B_T", [16, DI], BF16)
    convwf = sb("convwf", [P, NIC * 4], F32)
    convbf = sb("convbf", [P, NIC], F32)
    lng_sb = sb("lng_sb", [P, NDC], F32)
    lnb_sb = sb("lnb_sb", [P, NDC], F32)
    gbf = sb("gbf", [1, 8], F32)
    gbb = sb("gbb", [1, 8], BF16)
    onehot = sb("onehot_sb", [NEXP, NEXP * P], BF16)
    geF = sb("geF", [NEXP, C], BF16)
    gebc = sb("gebc", [P, 2 * C], BF16)
    probs = sb("probs", [P, 32], F32)
    ge_sb = sb("ge_sb", [P, 32], BF16)
    eqs = sb("eqs", [P, 32], F32)
    tts = sb("tts", [P, 32], F32)
    msk1 = sb("msk1", [P, 32], F32)
    msk2 = sb("msk2", [P, 32], F32)
    p2s = sb("p2s", [P, 32], F32)
    m1c = sb("m1c", [P, 4], F32)
    m2c = sb("m2c", [P, 4], F32)
    r1c = sb("r1c", [P, 4], F32)
    r2c = sb("r2c", [P, 4], F32)
    dinvc = sb("dinvc", [P, 4], F32)
    w1c = sb("w1c", [P, 4], F32)
    w2c = sb("w2c", [P, 4], F32)
    row_mean = sb("row_mean", [1, C], F32)
    row_a = sb("row_a", [1, C], F32)
    row_b = sb("row_b", [1, C], F32)
    s_C = sb("s_C", [16, C], BF16)
    ones_col_b = sb("ones_col_b", [P, 1], BF16)
    ones_row_b = sb("ones_row_b", [1, P], BF16)
    ones_row_f = sb("ones_row_f", [1, P], F32)
    eps_c = sb("eps_c", [1, 1], F32)
    ab_sb = sb("ab_sb", [P, 2 * C], BF16)

    pall = es.enter_context(nc.psum_tensor("pall", [P, 8 * C], F32))[:]

    def bank(i, p0=0, pn=P, c0=0, w=C):
        return pall[p0:p0 + pn, i * C + c0:i * C + c0 + w]

    def siluz(mc):
        return uni[:, mc * C:(mc + 1) * C]

    def xconv(mc):
        return uni[:, (NIC + mc) * C:(NIC + mc + 1) * C]

    def yattn(mc):
        return uni[:, (2 * NIC + mc) * C:(2 * NIC + mc + 1) * C]

    def hall(kc):
        return uni[:, kc * C:(kc + 1) * C]

    def xc(t, i):
        return t[:, i * C:(i + 1) * C]

    em = EM(nc, es)
    for s in ("t", "v", "s", "g", "sl0", "sl1", "sl2", "sl3", "xin", "xout",
              "cons", "dbg", "cast"):
        em.sem(s)

    T, V, S, SY, G = "tensor", "vector", "scalar", "sync", "gpsimd"

    def ld(dst, src_name, nk=None, nj=None):
        src = dr[src_name]
        if nk is not None:
            src = src.rearrange("(k p) j -> p k j", p=P)
            dst = dst.rearrange("p (k j) -> p k j", k=nk)
        return em.op(SY, lambda E, d=dst, s=src: E.dma_start(d, s), inc="cons", inc_by=16)

    _src30 = x_hbm.rearrange("(k p) t -> p k t", p=P)[:, :, 0:C]
    _dst30 = x2_sb[:, 0:NDC * C].rearrange("p (k t) -> p k t", k=NDC)
    ev_xload0 = em.op(SY, lambda E, d=_dst30, s=_src30: E.dma_start(d, s),
                      inc="xin", inc_by=16)

    ld(identB, "identB")
    ld(riota, "riota")
    ld(gwTf, "gwT", NDC, 8)
    ld(wxcf, "wxcT", NIC, 16)
    ld(x2_sb[0:16, NDC * C:NDC * C + DI], "blogT")
    ld(convwf, "convw")
    ld(convbf, "convb")
    ld(lng_sb, "lng")
    ld(lnb_sb, "lnb")
    ld(gbf, "gb")
    ld(onehot, "onehot")
    ev_cons = em.ev("cons")

    em.op(V, lambda E: E.memset(dec, 0.95))
    em.op(V, lambda E: E.memset(probs, -2.0))
    em.op(V, lambda E: E.memset(hist, 0.0))
    em.op(V, lambda E: E.memset(ones_col_b, 1.0))
    em.op(V, lambda E: E.memset(ones_row_b, 1.0))
    em.op(V, lambda E: E.memset(ones_row_f, 1.0))
    em.op(V, lambda E: E.memset(eps_c, float(EPS)))

    em.op(V, lambda E: E.memset(xin2, 0.0))
    em.op(V, lambda E: E.memset(hinit, 0.0))
    em.op(V, lambda E: E.tensor_copy(gwTb, gwTf), waits=[ev_cons])
    em.op(V, lambda E: E.tensor_copy(wxcb, wxcf))
    ev_vprep = em.op(V, lambda E: E.tensor_copy(gbb, gbf), inc="v")
    ev_bexp = em.op(S, lambda E: E.activation(B_T, x2_sb[0:16, NDC * C:NDC * C + DI],
                                              AF.Exp),
                    waits=[ev_cons], inc="s")

    set_srcs_cache = {}
    slot_cnt = [0, 0, 0, 0]
    ev_setdone = {}
    ev_setload = {}
    pending_g = [0]

    def set_srcs(s):
        if s >= 14:
            s -= 4
        if s == 0:
            return [("winTa", None, 0, 16384)]
        if s == 1:
            return [("winTb", None, 0, 16384)]
        if s == 2:
            return [("woutT", None, 0, 16384)]
        if 3 <= s <= 8:
            e = s - 3
            return [("ew1T", e, 0, 8192), ("ew3T", e, 8192, 8192)]
        if s == 9:
            return [("fc1T", None, 0, 8192), ("fc2T", None, 8192, 8192)]
        if 10 <= s <= 12:
            e0 = (s - 10) * 2
            return [("ew2T", e0, 0, 8192), ("ew2T", e0 + 1, 8192, 8192)]
        return [("fc3T", None, 0, 8192)]

    def set_off(s):
        if s >= 14:
            s -= 4
        return s * 16384 if s <= 13 else None

    def set_width(s):
        if s >= 14:
            s -= 4
        return 8192 if s == 13 else 16384

    ev_wb = {}
    ev_wb_all = [None]

    wb_next = [0]

    def issue_writeback(g):
        if not BF_CACHE or g < 2:
            return
        while wb_next[0] <= min(g - 1, 13):
            s2 = wb_next[0]
            wb_next[0] += 1
            si2 = s2 % 2
            wid = set_width(s2)
            off = set_off(s2)
            em.op(SY, lambda E, d=dr["wslab"][:, off:off + wid], sc=slot[si2][:, 0:wid]:
                  E.dma_start(d, sc), waits=[ev_setload[s2]], inc="cast", inc_by=16)
            ev_wb[s2] = em.ev("cast")
            if s2 == 13:
                ev_wb_all[0] = em.ev("cast")

    def issue_set(g):
        s = g % NSETS
        si = g % 2
        bf = BF_CACHE and g >= 2 * NSETS
        issue_writeback(g)
        if s >= 14 and g < 2 * NSETS:
            return ev_setload[g - 4]
        w = [ev_setdone[g - 2]] if g - 2 >= 0 else []
        if g - 2 in ev_wb:
            w.append(ev_wb[g - 2])
        if bf:
            assert ev_wb_all[0] is not None, "slab read before writebacks issued"
            w.append(ev_wb_all[0])
            off = set_off(s)
            wid = set_width(s)
            slot_cnt[si] += 16
            em.op(G, lambda E, d=slot[si][:, 0:wid], sc=dr["wslab"][:, off:off + wid]:
                  E.dma_start(d, sc), waits=w, inc=f"sl{si}", inc_by=16)
            return (f"sl{si}", slot_cnt[si])
        else:
            for j, (nm, e, dst0, width) in enumerate(set_srcs(s)):
                src = dr[nm]
                if e is not None:
                    src = src[e * D:(e + 1) * D, :]
                src = src.rearrange("(k p) j -> p k j", p=P)
                nk = src.shape[1]
                dst = slot[si][:, dst0:dst0 + width].rearrange("p (k j) -> p k j", k=nk)
                slot_cnt[si] += 16
                em.op(G, lambda E, d=dst, sc=src: E.dma_start(d, sc),
                      waits=w if j == 0 else (), inc=f"sl{si}", inc_by=16)
        return (f"sl{si}", slot_cnt[si])

    def ensure_sets(upto_g):
        upto_g = min(upto_g, NMACRO * NSETS - 1)
        while pending_g[0] <= upto_g:
            ev_setload[pending_g[0]] = issue_set(pending_g[0])
            pending_g[0] += 1

    def mark_setdone(g, ev):
        ev_setdone[g] = ev
        ensure_sets(g + 2)

    ensure_sets(1)

    ev_store = None
    ev_resid2 = [None] * NDC
    ev_xh_te_free = None
    ev_xbf_lastuse = None
    ev_bankfree = {}
    ev_hall = {}

    def ln_stats(src_bf, ev_src_per_chunk, sb, free_evs=()):
        ev_sq = [None] * NDC
        ev_stats_list = [None] * NDC
        te_entry = em.ev("t")
        for o in range(NDC):
            wsq = [ev_src_per_chunk[o],
                   te_entry if o < 2 else ev_stats_list[o - 2]]
            ev_sq[o] = em.op(S, lambda E, o=o: E.activation(
                xc(r2rot, o % 2), xc(src_bf, o), AF.Square), waits=wsq, inc="s")
            w = [ev_src_per_chunk[o], ev_vprep, ev_cons]
            if o == 0:
                w += list(free_evs)
            em.op(T, lambda E, o=o, sb=sb: E.matmul(
                bank(sb, 0, 1), ones_col_b, xc(src_bf, o),
                start=(o == 0), stop=(o == NDC - 1), skip_group_check=True), waits=w)
            ev_stats_list[o] = em.op(T, lambda E, o=o, sb=sb: E.matmul(
                bank(sb, 32, 1), ones_col_b, xc(r2rot, o % 2),
                start=(o == 0), stop=(o == NDC - 1), skip_group_check=True),
                waits=[ev_sq[o]], inc="t")
        ev_st = ev_stats_list[NDC - 1]
        e1 = em.op(V, lambda E, sb=sb: E.tensor_scalar(
            row_mean, bank(sb, 0, 1), 1.0 / D, None, OP.mult), waits=[ev_st], inc="v")
        e3 = em.op(V, lambda E: E.tensor_tensor(row_b, row_mean, row_mean, OP.mult),
                   waits=[e1], inc="v")
        e4 = em.op(V, lambda E, sb=sb: E.scalar_tensor_tensor(
            row_a, bank(sb, 32, 1), 1.0 / D, row_b, OP.mult, OP.subtract),
            waits=[e3], inc="v")
        e5 = em.op(S, lambda E: E.activation(row_b, row_a, AF.Sqrt, bias=eps_c),
                   waits=[e4], inc="s")
        e6 = em.op(V, lambda E: E.reciprocal(row_b, row_b), waits=[e5], inc="v")
        return (e1, e6)

    def ln_ab(rows_evs, ab, bb, mslot, extra_free=()):
        e1, e6 = rows_evs
        em.op(T, lambda E, bb=bb: E.matmul(bank(bb), ones_row_f, row_mean, start=True,
                                           stop=True, skip_group_check=True),
              waits=[e1] + list(extra_free))
        ev_t = em.op(T, lambda E, ab=ab: E.matmul(bank(ab), ones_row_f, row_b,
                                                  start=True, stop=True,
                                                  skip_group_check=True),
                     waits=[e6], inc="t")
        mt = ab_sb[:, 0:C]
        rt = ab_sb[:, C:2 * C]
        em.op(S, lambda E, mt=mt, bb=bb: E.copy(mt, bank(bb)), waits=[ev_t])
        ev_s = em.op(S, lambda E, rt=rt, ab=ab: E.copy(rt, bank(ab)), inc="s")
        return (ev_s, mt, rt)

    def ln_stats_and_ab(src_bf, ev_src_per_chunk, extra_b67_free=()):
        rows = ln_stats(src_bf, ev_src_per_chunk, 6,
                        free_evs=[ev_bankfree.get(6), ev_bankfree.get(7)]
                        + list(extra_b67_free))
        return ln_ab(rows, 6, 7)

    ev_store_par = [None, None]

    def emit_xload_casts_stats1(m, free_evs):
        par_m = m % 2
        xv = x2_sb[:, par_m * NDC * C:(par_m + 1) * NDC * C]
        xvc = [xv[:, i * C:(i + 1) * C] for i in range(NDC)]
        if m == 0:
            ev_xload = ev_xload0
        else:
            w = [ev_store_par[par_m]] if ev_store_par[par_m] is not None else []
            if m == 1:
                w.append(ev_bexp)
            src3 = x_hbm.rearrange("(k p) t -> p k t", p=P)[:, :, m * C:(m + 1) * C]
            dst3 = xv.rearrange("p (k t) -> p k t", k=NDC)
            ev_xload = em.op(SY, lambda E, d=dst3, s=src3: E.dma_start(d, s),
                             waits=w, inc="xin", inc_by=16)
        ev_cast1 = [None] * NDC
        for o in range(NDC):
            w = [ev_xload]
            if ev_xbf_lastuse is not None:
                w.append(ev_xbf_lastuse)
            ev_cast1[o] = em.op(S, lambda E, o=o, a=xvc[o]: E.copy(xc(xbf, o), a),
                                waits=w, inc="s")
        rows = ln_stats(xbf, ev_cast1, 2, free_evs=free_evs)
        return rows

    ln1_rows = None
    for m in range(NMACRO):
        g0 = m * NSETS
        seq_start = (m % SEQ_MACROS == 0)
        par_m = m % 2
        xv = x2_sb[:, par_m * NDC * C:(par_m + 1) * NDC * C]
        xvc = [xv[:, i * C:(i + 1) * C] for i in range(NDC)]
        src3 = x_hbm.rearrange("(k p) t -> p k t", p=P)[:, :, m * C:(m + 1) * C]
        dst3 = xv.rearrange("p (k t) -> p k t", k=NDC)

        if m == 0:
            ln1_rows = emit_xload_casts_stats1(0, [])

        ev_ab1, mt1, rt1 = ln_ab(ln1_rows, 2, 3, 0)
        w_xh_free = [ev_xh_te_free] if ev_xh_te_free is not None else []
        ev_sub1 = [None] * NDC
        ev_xh1 = [None] * NDC

        def ln1_sub(o):
            ev_sub1[o] = em.op(V, lambda E, o=o, mt=mt1: E.tensor_tensor(
                xc(xh, o), xc(xbf, o), mt, OP.subtract),
                waits=([ev_ab1] + w_xh_free) if o == 0 else (), inc="v")

        def ln1_mul(o):
            ev_xh1[o] = em.op(V, lambda E, o=o, rt=rt1: E.tensor_tensor(
                xc(xh, o), xc(xh, o), rt, OP.mult),
                waits=[ev_sub1[o]], inc="v")

        for o in range(NDC):
            ln1_sub(o)
            if o >= 1:
                ln1_mul(o - 1)
        ln1_mul(NDC - 1)
        ev_xhat_all = ev_xh1[NDC - 1]

        ev_sileva = [None] * NIC
        ev_zev = [None] * NIC
        ev_xinev = [None] * NIC
        ev_convdone = [None] * NIC
        for half in range(2):
            g = g0 + half
            for mc in range(16):
                pb = mc % 4
                w = [ev_setload[g]]
                if half == 0:
                    w.append(ev_xinev[mc - 4] if mc >= 4 else ev_bankfree.get(pb))
                else:
                    w.append(ev_zev[mc - 4] if mc >= 4 else ev_xinev[12 + mc])
                ev_mm = None
                for k in range(NDC):
                    lhs = slot[g % 2][:, k * 2048 + mc * P:k * 2048 + (mc + 1) * P]
                    ev_mm = em.op(T, lambda E, lhs=lhs, k=k, pb=pb: E.matmul(
                        bank(pb), lhs, xc(xh, k),
                        start=(k == 0), stop=(k == NDC - 1), skip_group_check=True),
                        waits=(w + [ev_xh1[k]]) if k == 0 else [ev_xh1[k]],
                        inc="t" if k == NDC - 1 else None)
                if half == 0:
                    par = mc % 2
                    wh = [ev_convdone[mc - 2]] if mc >= 2 else [ev_vprep]
                    if seq_start:
                        ev_hp = em.op(V, lambda E, par=par: E.memset(
                            xin2[:, par * 515:par * 515 + 3], 0.0), waits=wh, inc="v")
                    else:
                        ev_hp = em.op(V, lambda E, mc=mc, par=par: E.tensor_copy(
                            xin2[:, par * 515:par * 515 + 3],
                            hist[:, mc * 3:mc * 3 + 3]), waits=wh, inc="v")
                    ev_xinev[mc] = em.op(S, lambda E, par=par, pb=pb: E.copy(
                        xin2[:, par * 515 + 3:par * 515 + 515], bank(pb)),
                        waits=[ev_mm, ev_hp], inc="s")
                    em.op(V, lambda E, mc=mc, par=par: E.tensor_copy(
                        hist[:, mc * 3:mc * 3 + 3],
                        xin2[:, par * 515 + 512:par * 515 + 515]),
                        waits=[ev_xinev[mc]])
                    if mc % 2 == 1:
                        evs = {}
                        for tap in range(4):
                            for mm in (mc - 1, mc):
                                pp = mm % 2
                                rhs = xin2[:, pp * 515 + tap:pp * 515 + tap + 512]
                                cdst = cvt[:, pp * C:(pp + 1) * C]
                                if tap == 0:
                                    wcv = [ev_xinev[mm], ev_vprep]
                                    if mm >= 2:
                                        wcv.append(ev_sileva[mm - 2])
                                    evs[mm] = em.op(V, lambda E, rhs=rhs, cdst=cdst,
                                                    mm=mm: E.tensor_scalar(
                                        cdst, rhs, convwf[:, mm * 4:mm * 4 + 1],
                                        None, OP.mult), waits=wcv, inc="v")
                                else:
                                    evs[mm] = em.op(V, lambda E, rhs=rhs, cdst=cdst,
                                                    mm=mm, tap=tap:
                                                    E.scalar_tensor_tensor(
                                        cdst, rhs,
                                        convwf[:, mm * 4 + tap:mm * 4 + tap + 1],
                                        cdst, OP.mult, OP.add),
                                        waits=[evs[mm]], inc="v")
                        for mm in (mc - 1, mc):
                            ppx = mm % 2
                            ev_convdone[mm] = evs[mm]
                            ev_sileva[mm] = em.op(S, lambda E, mm=mm,
                                                  cdst=cvt[:, (mm % 2) * C:
                                                           ((mm % 2) + 1) * C]:
                                                  E.activation(
                                xconv(mm), cdst, AF.Silu,
                                bias=convbf[:, mm:mm + 1]), waits=[evs[mm]], inc="s")
                else:
                    ev_zev[mc] = em.op(S, lambda E, mc=mc, pb=pb: E.activation(
                        siluz(mc), bank(pb), AF.Silu), waits=[ev_mm], inc="s")
            mark_setdone(g, em.ev("t"))

        if TRUNC < 2:
            ev_resid2 = [None] * NDC
            ev_store = em.op(SY, lambda E, m=m, s=dst3: E.dma_start(
                out_hbm.rearrange("(k p) t -> p k t", p=P)[:, :, m * C:(m + 1) * C], s),
                waits=[em.ev("v"), em.ev("s"), em.ev("t")], inc="xout", inc_by=16)
            ev_store_par[par_m] = ev_store
            ev_xh_te_free = em.ev("t")
            ev_xbf_lastuse = em.ev("t")
            for o in range(NDC):
                ev_resid2[o] = em.ev("v")
            ev_bankfree = {}
            for gg in range(g0, g0 + NSETS):
                if gg not in ev_setdone:
                    mark_setdone(gg, em.ev("t"))
            continue

        ev_wx = None
        for k in range(NIC):
            w = [ev_sileva[k], ev_vprep]
            if k == 0:
                w.append(ev_bankfree.get(4))
            ev_wx = em.op(T, lambda E, k=k: E.matmul(
                bank(4, 0, 16), wxcb[:, k * 16:(k + 1) * 16], xconv(k),
                start=(k == 0), stop=(k == NIC - 1), skip_group_check=True),
                waits=w, inc="t" if k == NIC - 1 else None)
        ev_sc = em.op(V, lambda E: E.tensor_copy(s_C, bank(4, 0, 16)),
                      waits=[ev_wx, ev_bexp], inc="v")

        g = g0 + 2
        ev_cg = [None] * NIC
        ev_scan = [None] * NIC
        ev_g2 = [None] * NIC
        ev_y = [None] * NIC
        ev_wo_stop = [None] * NDC
        ev_resid1 = [None] * NDC

        def emit_cg(mc):
            cgb = 6 + mc % 2
            w = [ev_sc, ev_xhat_all]
            if mc >= 2:
                w.append(ev_g2[mc - 2])
            else:
                w.append(ev_bankfree.get(cgb))
            ev_cg[mc] = em.op(T, lambda E, mc=mc, cgb=cgb: E.matmul(
                bank(cgb), B_T[:, mc * P:(mc + 1) * P], s_C,
                start=True, stop=True, skip_group_check=True), waits=w, inc="t")

        def emit_wout_col(c):
            for o in range(6):
                lhs = slot[g % 2][:, c * 1024 + o * P:c * 1024 + (o + 1) * P]
                w = []
                if o == 0:
                    w = [ev_y[c], ev_setload[g]]
                    if c == 0:
                        w += [ev_zev[12], ev_zev[13], ev_zev[14], ev_zev[15],
                              ev_sc, ev_bankfree.get(5)]
                ev = em.op(T, lambda E, lhs=lhs, c=c, o=o: E.matmul(
                    bank(o), lhs, yattn(c), start=(c == 0), stop=(c == NIC - 1),
                    skip_group_check=True), waits=w,
                    inc="t" if c == NIC - 1 else None)
                if c == NIC - 1:
                    ev_wo_stop[o] = ev

        for mc in range(NIC):
            emit_cg(mc)
            par = mc % 2
            w = [ev_sileva[mc]]
            if mc >= 2:
                w.append(ev_y[mc - 2])
            init = 0.0 if seq_start else hinit[:, mc:mc + 1]
            ev_scan[mc] = em.op(V, lambda E, mc=mc, par=par, init=init:
                                E.tensor_tensor_scan(
                xc(h2, par), dec, xconv(mc), init, OP.mult, OP.add), waits=w, inc="v")
            ev_g2[mc] = em.op(V, lambda E, mc=mc, par=par: E.tensor_tensor(
                xc(g2b, par), bank(6 + mc % 2), siluz(mc), OP.mult),
                waits=[ev_cg[mc], ev_zev[mc]], inc="v")
            ev_y[mc] = em.op(G, lambda E, mc=mc, par=par: E.tensor_tensor(
                yattn(mc), xc(h2, par), xc(g2b, par), OP.mult),
                waits=[ev_scan[mc], ev_g2[mc]], inc="g")
            em.op(V, lambda E, mc=mc, par=par: E.tensor_copy(
                hinit[:, mc:mc + 1], h2[:, par * C + C - 1:par * C + C]),
                waits=[ev_scan[mc]])
            if mc >= 2:
                emit_wout_col(mc - 2)
        emit_wout_col(NIC - 2)
        emit_wout_col(NIC - 1)
        ev_y_all = em.ev("v")
        for o in (6, 7):
            for c in range(NIC):
                lhs = slot[g % 2][:, c * 1024 + o * P:c * 1024 + (o + 1) * P]
                w = []
                if c == 0:
                    w = [ev_y_all, ev_g2[14], ev_g2[15]]
                ev = em.op(T, lambda E, lhs=lhs, c=c, o=o: E.matmul(
                    bank(o), lhs, yattn(c), start=(c == 0), stop=(c == NIC - 1),
                    skip_group_check=True), waits=w,
                    inc="t" if c == NIC - 1 else None)
                if c == NIC - 1:
                    ev_wo_stop[o] = ev
        for o in range(NDC):
            ev_resid1[o] = em.op(V, lambda E, o=o: E.tensor_tensor(
                xc(xh, o), xc(xh, o), bank(o), OP.add),
                waits=[ev_wo_stop[o]], inc="v")
        mark_setdone(g, em.ev("t"))

        if DEBUG_TAPS:
            for mc in range(NIC):
                em.op(G, lambda E, mc=mc, m=m: E.dma_start(
                    dbg_y[mc * P:(mc + 1) * P, m * C:(m + 1) * C], yattn(mc)),
                    waits=[ev_y[mc]], inc="dbg", inc_by=16)


        if DEBUG_TAPS:
            for o in range(NDC):
                em.op(G, lambda E, o=o, m=m: E.dma_start(
                    dbg_r[o * P:(o + 1) * P, m * C:(m + 1) * C], xc(xh, o)),
                    waits=[ev_resid1[o]], inc="dbg", inc_by=16)

        rows2 = ln_stats(xh, ev_resid1, 6,
                         free_evs=[ev_bankfree.get(6), ev_bankfree.get(7),
                                   ev_resid1[6], ev_resid1[7]])
        ev_ab2, mt2, rt2 = ln_ab(rows2, 6, 7, 1)
        ev_t1 = [None] * NDC
        ev_t2 = [None] * NDC
        ev_t3 = [None] * NDC
        ev_resid_out = [None] * NDC
        ev_cast2 = [None] * NDC

        def inner_t1(o):
            ev_t1[o] = em.op(V, lambda E, o=o, mt=mt2: E.tensor_tensor(
                xc(xbf, o), xc(xh, o), mt, OP.subtract),
                waits=[ev_ab2] if o == 0 else [ev_t1[o - 1]], inc="v")

        def inner_t2(o):
            ev_t2[o] = em.op(V, lambda E, o=o, rt=rt2: E.tensor_tensor(
                xc(xbf, o), xc(xbf, o), rt, OP.mult), waits=[ev_t1[o]], inc="v")

        def inner_t3(o):
            ev_t3[o] = em.op(S, lambda E, o=o: E.activation(
                xc(xbf, o), xc(xbf, o), AF.Identity,
                bias=lnb_sb[:, o:o + 1], scale=lng_sb[:, o:o + 1]),
                waits=[ev_t2[o]], inc="s")

        def inner_resid(o):
            ev_resid_out[o] = em.op(G, lambda E, o=o, a=xvc[o]: E.tensor_tensor(
                a, a, xc(xbf, o), OP.add), waits=[ev_t3[o]], inc="g")
            ev_cast2[o] = em.op(S, lambda E, o=o, a=xvc[o]: E.copy(xc(xbf, o), a),
                                waits=[ev_resid_out[o]], inc="s")

        for o in range(NDC):
            inner_t1(o)
            if o >= 1:
                inner_t2(o - 1)
            if o >= 2:
                inner_t3(o - 2)
            if o >= 3:
                inner_resid(o - 3)
        inner_t2(NDC - 1)
        inner_t3(NDC - 2)
        inner_t3(NDC - 1)
        for o in range(NDC - 3, NDC):
            inner_resid(o)

        rows3 = ln_stats(xbf, ev_cast2, 6, free_evs=[ev_ab2])
        ev_ab3, mt3, rt3 = ln_ab(rows3, 6, 7, 2)
        ev_xbf_lastuse = em.ev("t")
        ev_st2pc = [None] * NDC
        ev_xh2pc = [None] * NDC

        def ln2_sub(o):
            ev_st2pc[o] = em.op(V, lambda E, o=o, mt=mt3: E.tensor_tensor(
                xc(xh, o), xc(xbf, o), mt, OP.subtract),
                waits=[ev_ab3] if o == 0 else (), inc="v")

        def ln2_mul(o):
            ev_xh2pc[o] = em.op(V, lambda E, o=o, rt=rt3: E.tensor_tensor(
                xc(xh, o), xc(xh, o), rt, OP.mult),
                waits=[ev_st2pc[o]], inc="v")

        for o in range(NDC):
            ln2_sub(o)
            if o >= 1:
                ln2_mul(o - 1)
        ln2_mul(NDC - 1)
        ev_xh2_all = ev_xh2pc[NDC - 1]

        ev_lg = [None] * 4
        for tc in range(4):
            w = [ev_vprep]
            for k in range(NDC):
                lhs = xh[:, k * C + tc * P:k * C + (tc + 1) * P]
                em.op(T, lambda E, lhs=lhs, tc=tc, k=k: E.matmul(
                    bank(6, 0, P, tc * 8, 6), lhs, gwTb[:, k * 8:k * 8 + 6],
                    start=(k == 0), stop=False, skip_group_check=True),
                    waits=(w + [ev_xh2pc[k]]) if k == 0 else [ev_xh2pc[k]])
            ev_lg[tc] = em.op(T, lambda E, tc=tc: E.matmul(
                bank(6, 0, P, tc * 8, 6), ones_row_b, gbb[0:1, 0:6],
                start=False, stop=True, skip_group_check=True), inc="t")
        ev_sig = [None] * 4
        for tc in range(4):
            ev_sig[tc] = em.op(S, lambda E, tc=tc: E.activation(
                probs[:, tc * 8:tc * 8 + 6], bank(6, 0, P, tc * 8, 6), AF.Sigmoid),
                waits=[ev_lg[3]], inc="s")

        if TRUNC == 22:
            ev_resid2 = [None] * NDC
            ev_store = em.op(SY, lambda E, m=m, s=dst3: E.dma_start(
                out_hbm.rearrange("(k p) t -> p k t", p=P)[:, :, m * C:(m + 1) * C], s),
                waits=[em.ev("v"), em.ev("s"), em.ev("t")], inc="xout", inc_by=16)
            ev_store_par[par_m] = ev_store
            ev_xh_te_free = em.ev("t")
            ev_xbf_lastuse = em.ev("t")
            for o in range(NDC):
                ev_resid2[o] = em.ev("v")
            ev_bankfree = {}
            for gg in range(g0, g0 + NSETS):
                if gg not in ev_setdone:
                    mark_setdone(gg, em.ev("t"))
            continue

        probs3 = probs.rearrange("p (a b) -> p a b", a=4)
        p2s3 = p2s.rearrange("p (a b) -> p a b", a=4)
        eqs3 = eqs.rearrange("p (a b) -> p a b", a=4)
        tts3 = tts.rearrange("p (a b) -> p a b", a=4)
        msk13 = msk1.rearrange("p (a b) -> p a b", a=4)
        msk23 = msk2.rearrange("p (a b) -> p a b", a=4)

        def bc4(t):
            return bass.AP(t.tensor, t.offset, [[4, P], [1, 4], [0, 8]])

        riota_b = bass.AP(riota.tensor, riota.offset, [[8, P], [0, 4], [1, 8]])
        ge3 = ge_sb.rearrange("p (a b) -> p a b", a=4)
        ev = em.op(V, lambda E: E.reduce_max(m1c, probs3, axis=AX.X),
                   waits=[ev_sig[0], ev_sig[1], ev_sig[2], ev_sig[3]], inc="v")
        ev = em.op(V, lambda E: E.tensor_tensor(eqs3, probs3, bc4(m1c), OP.is_equal),
                   waits=[ev], inc="v")
        ev = em.op(V, lambda E: E.tensor_tensor(tts3, eqs3, riota_b, OP.mult),
                   waits=[ev], inc="v")
        ev = em.op(V, lambda E: E.reduce_max(r1c, tts3, axis=AX.X), waits=[ev], inc="v")
        ev = em.op(V, lambda E: E.tensor_tensor(msk13, tts3, bc4(r1c), OP.is_equal),
                   waits=[ev], inc="v")
        ev = em.op(V, lambda E: E.scalar_tensor_tensor(p2s, msk1, -1e30, probs,
                                                       OP.mult, OP.add),
                   waits=[ev], inc="v")
        ev = em.op(V, lambda E: E.reduce_max(m2c, p2s3, axis=AX.X), waits=[ev], inc="v")
        ev = em.op(V, lambda E: E.tensor_tensor(eqs3, p2s3, bc4(m2c), OP.is_equal),
                   waits=[ev], inc="v")
        ev = em.op(V, lambda E: E.tensor_tensor(tts3, eqs3, riota_b, OP.mult),
                   waits=[ev], inc="v")
        ev = em.op(V, lambda E: E.reduce_max(r2c, tts3, axis=AX.X), waits=[ev], inc="v")
        ev_mk2 = em.op(V, lambda E: E.tensor_tensor(msk23, tts3, bc4(r2c), OP.is_equal),
                       waits=[ev], inc="v")
        ev = em.op(V, lambda E: E.scalar_tensor_tensor(dinvc, m1c, 1e-8, m2c,
                                                       OP.add, OP.add),
                   waits=[ev], inc="v")
        ev = em.op(V, lambda E: E.reciprocal(dinvc, dinvc), waits=[ev], inc="v")
        ev_w1 = em.op(V, lambda E: E.tensor_tensor(w1c, m1c, dinvc, OP.mult),
                      waits=[ev], inc="v")
        ev_w2 = em.op(V, lambda E: E.tensor_tensor(w2c, m2c, dinvc, OP.mult),
                      waits=[ev_w1], inc="v")
        ev = em.op(V, lambda E: E.tensor_tensor(eqs3, msk13, bc4(w1c), OP.mult),
                   waits=[ev_w1], inc="v")
        ev = em.op(V, lambda E: E.tensor_tensor(tts3, msk23, bc4(w2c), OP.mult),
                   waits=[ev_w2, ev_mk2], inc="v")
        ev_ge_all = em.op(V, lambda E: E.tensor_tensor(ge_sb, eqs, tts, OP.add),
                          waits=[ev], inc="v")

        geF_state = {}

        def emit_geT():
            ev_tr = None
            for tc in range(4):
                ev_tr = em.op(T, lambda E, tc=tc: E.matmul(
                    bank(7, 0, 6, tc * P, P), ge_sb[:, tc * 8:tc * 8 + 6], identB,
                    start=True, stop=True, skip_group_check=True),
                    waits=[ev_ge_all] if tc == 0 else (), inc="t")
            for tc in range(4):
                em.op(V, lambda E, tc=tc: E.tensor_copy(
                    geF[:, tc * P:(tc + 1) * P], bank(7, 0, 6, tc * P, P)),
                    waits=[ev_tr], inc="v" if tc == 3 else None)
            geF_state["ev"] = em.ev("v")

        if TRUNC < 4:
            ev_resid2 = [None] * NDC
            ev_store = em.op(SY, lambda E, m=m, s=dst3: E.dma_start(
                out_hbm.rearrange("(k p) t -> p k t", p=P)[:, :, m * C:(m + 1) * C], s),
                waits=[em.ev("v"), em.ev("s"), em.ev("t")], inc="xout", inc_by=16)
            ev_store_par[par_m] = ev_store
            ev_xh_te_free = em.ev("t")
            ev_xbf_lastuse = em.ev("t")
            for o in range(NDC):
                ev_resid2[o] = em.ev("v")
            ev_bankfree = {}
            for gg in range(g0, g0 + NSETS):
                if gg not in ev_setdone:
                    mark_setdone(gg, em.ev("t"))
            continue

        last_h_by_bank = {0: ev_resid1[0], 1: ev_resid1[1],
                          2: ev_resid1[2], 3: ev_resid1[3]}
        last_ssil = [None, None]
        ev_gebc_done = [None, None]
        for e in range(7):
            g = g0 + (3 + e if e < 6 else 9)
            epar = e % 2
            ev_mm1_l = [None] * NDC
            ev_mm3_l = [None] * NDC

            def emit_h_mms(mc, e=e, g=g):
                b1 = mc % 2
                b3 = 2 + mc % 2
                w1 = [ev_xh2_all, ev_setload[g], last_h_by_bank[b1]]
                w3 = [last_h_by_bank[b3]]
                ev = None
                for k in range(NDC):
                    lhs = slot[g % 2][:, k * 1024 + mc * P:k * 1024 + (mc + 1) * P]
                    ev = em.op(T, lambda E, lhs=lhs, k=k, b1=b1: E.matmul(
                        bank(b1), lhs, xc(xh, k), start=(k == 0), stop=(k == NDC - 1),
                        skip_group_check=True), waits=w1 if k == 0 else (),
                        inc="t" if k == NDC - 1 else None)
                ev_mm1_l[mc] = ev
                for k in range(NDC):
                    lhs = slot[g % 2][:, 8192 + k * 1024 + mc * P:
                                      8192 + k * 1024 + (mc + 1) * P]
                    ev = em.op(T, lambda E, lhs=lhs, k=k, b3=b3: E.matmul(
                        bank(b3), lhs, xc(xh, k), start=(k == 0), stop=(k == NDC - 1),
                        skip_group_check=True), waits=w3 if k == 0 else (),
                        inc="t" if k == NDC - 1 else None)
                ev_mm3_l[mc] = ev
                last_h_by_bank[b1] = ("pending", mc)
                last_h_by_bank[b3] = ("pending", mc)

            def emit_gebc(e=e, epar=epar):
                gb_bank = 4 + epar
                w = [geF_state["ev"], ev_cons, ev_resid1[4 + epar],
                     ev_gebc_done[epar]]
                ev_bc = em.op(T, lambda E, e=e, gb_bank=gb_bank: E.matmul(
                    bank(gb_bank), onehot[:, e * P:(e + 1) * P], geF,
                    start=True, stop=True, skip_group_check=True), waits=w, inc="t")
                ev_gebc_done[epar] = em.op(V, lambda E, epar=epar, gb_bank=gb_bank:
                                           E.tensor_copy(xc(gebc, epar), bank(gb_bank)),
                                           waits=[ev_bc], inc="v")

            def resolve_pending():
                for b in (0, 1, 2, 3):
                    v = last_h_by_bank[b]
                    if isinstance(v, tuple) and v[0] == "pending":
                        kcp = e * 8 + v[1]
                        if kcp in ev_hall and ev_hall[kcp] is not None:
                            last_h_by_bank[b] = ev_hall[kcp]

            emit_h_mms(0)
            emit_h_mms(1)
            if e == 0:
                emit_geT()
            if e < 6:
                emit_gebc()
            for mc in range(NDC):
                kc = e * 8 + mc
                b1 = mc % 2
                b3 = 2 + mc % 2
                spar = mc % 2
                ev_sl = em.op(S, lambda E, spar=spar, b1=b1: E.activation(
                    xc(ssil, spar), bank(b1), AF.Silu),
                    waits=[ev_mm1_l[mc], last_ssil[spar]], inc="s")
                if e < 6:
                    ev_op1 = em.op(V, lambda E, kc=kc, b3=b3, epar=epar: E.tensor_tensor(
                        hall(kc), bank(b3), xc(gebc, epar), OP.mult),
                        waits=[ev_mm3_l[mc], ev_gebc_done[epar]], inc="v")
                    ev_hall[kc] = em.op(G, lambda E, kc=kc, spar=spar: E.tensor_tensor(
                        hall(kc), hall(kc), xc(ssil, spar), OP.mult),
                        waits=[ev_op1, ev_sl], inc="g")
                else:
                    ev_op1 = em.op(V, lambda E, kc=kc, spar=spar, b3=b3:
                                   E.tensor_tensor(
                        hall(kc), bank(b3), xc(ssil, spar), OP.mult),
                        waits=[ev_mm3_l[mc], ev_sl], inc="v")
                    ev_hall[kc] = ev_op1
                last_ssil[spar] = ev_hall[kc]
                if mc + 2 < NDC:
                    last_h_by_bank[(mc + 2) % 2] = ev_sl
                    last_h_by_bank[2 + (mc + 2) % 2] = ev_op1
                    emit_h_mms(mc + 2)
                else:
                    last_h_by_bank[b1] = ev_sl
                    last_h_by_bank[b3] = ev_op1
            mark_setdone(g, em.ev("t"))

        if TRUNC < 5:
            ev_resid2 = [None] * NDC
            ev_store = em.op(SY, lambda E, m=m, s=dst3: E.dma_start(
                out_hbm.rearrange("(k p) t -> p k t", p=P)[:, :, m * C:(m + 1) * C], s),
                waits=[em.ev("v"), em.ev("s"), em.ev("t")], inc="xout", inc_by=16)
            ev_store_par[par_m] = ev_store
            ev_xh_te_free = em.ev("t")
            ev_xbf_lastuse = em.ev("t")
            for o in range(NDC):
                ev_resid2[o] = em.ev("v")
            ev_bankfree = {}
            for gg in range(g0, g0 + NSETS):
                if gg not in ev_setdone:
                    mark_setdone(gg, em.ev("t"))
            continue

        ev_ph2_start_v = em.ev("v")
        ev_ph2_start_s = em.ev("s")

        def ph2_half(half):
            ev_ph2 = None
            base = g0 + (10 if half == 0 else 14)
            for kc in range(NKC):
                if kc < 48:
                    g = base + kc // 16
                    off = ((kc // 8) % 2) * 8192 + (kc % 8) * 1024
                else:
                    g = base + 3
                    off = (kc - 48) * 1024
                for o in range(half * 4, half * 4 + 4):
                    lhs = slot[g % 2][:, off + o * P:off + (o + 1) * P]
                    w = []
                    if o == half * 4:
                        w = [ev_hall[kc], ev_setload[g]]
                        if kc == 0:
                            w += [ev_ph2_start_v, ev_ph2_start_s]
                    ev_ph2 = em.op(T, lambda E, lhs=lhs, kc=kc, o=o: E.matmul(
                        bank(o), lhs, hall(kc), start=(kc == 0), stop=(kc == NKC - 1),
                        skip_group_check=True), waits=w,
                        inc="t" if kc == NKC - 1 else None)
                if kc in (15, 31, 47, 55):
                    mark_setdone(g, em.ev("t"))
            return ev_ph2

        ev_ph2a = ph2_half(0)
        for o in range(4):
            ev_resid2[o] = em.op(V, lambda E, o=o, a=xvc[o]: E.tensor_tensor(
                a, a, bank(o), OP.add), waits=[ev_ph2a], inc="v")
            ev_bankfree[o] = ev_resid2[o]
        if m + 1 < NMACRO:
            ln1_rows = emit_xload_casts_stats1(
                m + 1, [ev_resid2[2], ev_resid2[3]])
        ev_ph2b = ph2_half(1)
        for o in range(4, NDC):
            ev_resid2[o] = em.op(V, lambda E, o=o, a=xvc[o]: E.tensor_tensor(
                a, a, bank(o), OP.add), waits=[ev_ph2b], inc="v")
            ev_bankfree[o] = ev_resid2[o]
        dsto = out_hbm.rearrange("(k p) t -> p k t", p=P)[:, :, m * C:(m + 1) * C]
        ev_store = em.op(SY, lambda E, d=dsto, s=dst3: E.dma_start(d, s),
            waits=[ev_resid2[o] for o in range(NDC)], inc="xout", inc_by=16)
        ev_store_par[par_m] = ev_store

        ev_xh_te_free = em.ev("t")

    em.wait(SY, em.ev("xout"))

    with nc.Block() as block:

        @block.tensor
        def _(E):
            for f in em.ops["tensor"]:
                f(E)

        @block.vector
        def _(E):
            for f in em.ops["vector"]:
                f(E)

        @block.scalar
        def _(E):
            for f in em.ops["scalar"]:
                f(E)

        @block.sync
        def _(E):
            for f in em.ops["sync"]:
                f(E)

        @block.gpsimd
        def _(E):
            for f in em.ops["gpsimd"]:
                f(E)

    es.close()
    return nc


def _onehot_const():
    oh = np.zeros((NEXP, NEXP * P), dtype=ml_dtypes.bfloat16)
    for e in range(NEXP):
        oh[e, e * P:(e + 1) * P] = 1.0
    return oh


def kernel(**inputs):
    f32 = np.float32
    xin = np.asarray(inputs["L0_attn_x"], f32)
    W_in = np.asarray(inputs["L0_attn_W_in"], f32)
    conv_w = np.asarray(inputs["L0_attn_conv_w"], f32)
    conv_b = np.asarray(inputs["L0_attn_conv_b"], f32)
    W_x = np.asarray(inputs["L0_attn_W_x"], f32)
    B_log = np.asarray(inputs["L0_attn_B_log"], f32)
    W_out = np.asarray(inputs["L0_attn_W_out"], f32)
    ln_g = np.asarray(inputs["L0_attn_ln_g"], f32)
    ln_b = np.asarray(inputs["L0_attn_ln_b"], f32)
    gate_w = np.asarray(inputs["L0_ffn_gate_w"], f32)
    gate_b = np.asarray(inputs["L0_ffn_gate_b"], f32)
    ew1 = np.asarray(inputs["L0_ffn_ew1"], f32)
    ew2 = np.asarray(inputs["L0_ffn_ew2"], f32)
    ew3 = np.asarray(inputs["L0_ffn_ew3"], f32)
    fc1 = np.asarray(inputs["L0_ffn_fc1"], f32)
    fc2 = np.asarray(inputs["L0_ffn_fc2"], f32)
    fc3 = np.asarray(inputs["L0_ffn_fc3"], f32)

    winT = np.ascontiguousarray(W_in.T)
    shared = {
        "winTa": np.ascontiguousarray(winT[:, :DI]),
        "winTb": np.ascontiguousarray(winT[:, DI:]),
        "woutT": np.ascontiguousarray(W_out.T),
        "ew1T": np.ascontiguousarray(np.transpose(ew1, (0, 2, 1)).reshape(NEXP * D, D)),
        "ew3T": np.ascontiguousarray(np.transpose(ew3, (0, 2, 1)).reshape(NEXP * D, D)),
        "ew2T": np.ascontiguousarray(np.transpose(ew2, (0, 2, 1)).reshape(NEXP * D, D)),
        "fc1T": np.ascontiguousarray(fc1.T),
        "fc2T": np.ascontiguousarray(fc2.T),
        "fc3T": np.ascontiguousarray(fc3.T),
        "wxcT": np.ascontiguousarray(W_x[16:32].T),
        "blogT": np.ascontiguousarray(B_log.T),
        "gwT": np.ascontiguousarray(
            np.concatenate([gate_w.T, np.zeros((D, 2), f32)], axis=1)),
        "gb": np.concatenate([gate_b, np.zeros(2, f32)]).reshape(1, 8),
        "convw": np.ascontiguousarray(
            conv_w[:, 0, :].reshape(NIC, P, 4).transpose(1, 0, 2).reshape(P, NIC * 4)),
        "convb": np.ascontiguousarray(conv_b.reshape(NIC, P).T),
        "lng": np.ascontiguousarray(ln_g.reshape(NDC, P).T),
        "lnb": np.ascontiguousarray(ln_b.reshape(NDC, P).T),
        "identB": np.eye(P, dtype=ml_dtypes.bfloat16),
        "riota": np.broadcast_to(8.0 - np.arange(8, dtype=f32), (P, 8)).copy(),
        "onehot": _onehot_const(),
    }

    n = 8
    in_maps = []
    for c in range(n):
        xc_ = xin[2 * c:2 * c + 2].reshape(TOK, D)
        m = dict(shared)
        m["x"] = np.ascontiguousarray(xc_.T)
        in_maps.append(m)

    nc = build_nc()
    res = run_bass_kernel_spmd(nc, in_maps, core_ids=list(range(n)))
    outs = []
    for c in range(n):
        o = np.asarray(res.results[c]["out"])
        outs.append(np.ascontiguousarray(o.T).reshape(2, 4096, D))
    return np.concatenate(outs, axis=0)


if __name__ == "__main__":
    np.random.seed(0)
    import reference as _r
    ins = _r.setup_inputs(0)
    ins = {k: np.asarray(v) for k, v in ins.items()}
    out = kernel(**ins)
    ref = np.asarray(_r.reference(**ins))
    num = float(((out - ref) ** 2).mean())
    den = float((ref ** 2).mean())
    print("resid-var ratio", num / den)


# version 33 (final)
# speedup vs baseline: 1.0488x; 1.0488x over previous
import contextlib
import numpy as np
import ml_dtypes
import concourse.bass as bass
import concourse.mybir as mybir
from concourse.bass_utils import run_bass_kernel_spmd

F32 = mybir.dt.float32
BF16 = mybir.dt.bfloat16
AF = mybir.ActivationFunctionType
OP = mybir.AluOpType
AX = mybir.AxisListType

P = 128
C = 512
NMACRO = 16
SEQ_MACROS = 8
TOK = C * NMACRO
D = 1024
DI = 2048
NDC = D // P
NIC = DI // P
NEXP = 6
NKC = 56
EPS = 1e-5

ENGINES = ("tensor", "vector", "scalar", "sync", "gpsimd")
NSETS = 18
DEBUG_TAPS = False
BF_CACHE = True
TRUNC = 5


class EM:

    def __init__(self, nc, es):
        self.nc = nc
        self.es = es
        self.ops = {k: [] for k in ENGINES}
        self.semh = {}
        self.cnt = {}
        self.lastw = {}

    def sem(self, name):
        self.semh[name] = self.es.enter_context(self.nc.semaphore(name))
        self.cnt[name] = 0

    def wait(self, eng, ev):
        if ev is None:
            return
        nm, val = ev
        key = (eng, nm)
        if self.lastw.get(key, 0) >= val:
            return
        self.lastw[key] = val
        h = self.semh[nm]
        self.ops[eng].append(lambda E, h=h, v=val: E.wait_ge(h, v))

    def op(self, eng, fn, waits=(), inc=None, inc_by=1):
        for ev in waits:
            self.wait(eng, ev)
        if inc is not None:
            self.cnt[inc] += inc_by
            ev_out = (inc, self.cnt[inc])
            h = self.semh[inc]
            self.ops[eng].append(lambda E, fn=fn, h=h, b=inc_by: fn(E).then_inc(h, b))
            return ev_out
        self.ops[eng].append(lambda E, fn=fn: fn(E))
        return None

    def ev(self, name):
        return (name, self.cnt[name])


def build_nc():
    nc = bass.Bass("TRN2", target_bir_lowering=False, detect_race_conditions=False)

    dr = {}

    def din(name, shape, dt=F32):
        dr[name] = nc.dram_tensor(name, list(shape), dt, kind="ExternalInput").ap()
        return dr[name]

    x_hbm = din("x", [D, TOK])
    out_hbm = nc.dram_tensor("out", [D, TOK], F32, kind="ExternalOutput").ap()
    if DEBUG_TAPS:
        dbg_y = nc.dram_tensor("dbg_y", [NIC * P, TOK], F32, kind="ExternalOutput").ap()
        dbg_r = nc.dram_tensor("dbg_r", [D, TOK], F32, kind="ExternalOutput").ap()

    din("winTa", [D, DI])
    din("winTb", [D, DI])
    din("woutT", [DI, D])
    din("ew1T", [NEXP * D, D])
    din("ew3T", [NEXP * D, D])
    din("ew2T", [NEXP * D, D])
    din("fc1T", [D, D])
    din("fc2T", [D, D])
    din("fc3T", [D, D])
    din("wxcT", [DI, 16])
    din("blogT", [16, DI])
    din("gwT", [D, 8])
    din("gb", [1, 8])
    din("convw", [P, NIC * 4])
    din("convb", [P, NIC])
    din("lng", [P, NDC])
    din("lnb", [P, NDC])
    din("identB", [P, P], BF16)
    WSLAB_COLS = 13 * 16384 + 8192
    dr["wslab"] = nc.dram_tensor("wslab", [P, WSLAB_COLS], BF16, kind="Internal").ap()
    din("riota", [P, 8])
    din("onehot", [NEXP, NEXP * P], BF16)

    es = contextlib.ExitStack()

    def sb(name, shape, dt):
        return es.enter_context(nc.sbuf_tensor(name, list(shape), dt))[:]

    x2_sb = sb("x2_sb", [P, 2 * NDC * C], F32)
    x_sb = x2_sb[:, 0:NDC * C]
    xbf = sb("xbf", [P, NDC * C], BF16)
    xh = sb("xh", [P, NDC * C], BF16)
    uni = sb("uni", [P, NKC * C], BF16)
    slot = [sb(f"slot{i}", [P, 16384], BF16) for i in range(2)]
    xin2 = sb("xin2", [P, 2 * (C + 3)], BF16)
    cvt = sb("cvt", [P, 2 * C], BF16)
    h2 = sb("h2", [P, 2 * C], F32)
    dec = sb("dec", [P, C], F32)
    g2b = sb("g2b", [P, 2 * C], BF16)
    r2rot = sb("r2rot", [P, 2 * C], BF16)
    ssil = sb("ssil", [P, 2 * C], BF16)
    hist = sb("hist", [P, NIC * 3], BF16)
    hinit = sb("hinit", [P, NIC], F32)
    identB = sb("identB_sb", [P, P], BF16)
    riota = sb("riota_sb", [P, 8], F32)
    gwTf = sb("gwTf", [P, NDC * 8], F32)
    gwTb = sb("gwTb", [P, NDC * 8], BF16)
    wxcf = sb("wxcf", [P, NIC * 16], F32)
    wxcb = sb("wxcb", [P, NIC * 16], BF16)
    B_T = sb("B_T", [16, DI], BF16)
    convwf = sb("convwf", [P, NIC * 4], F32)
    convbf = sb("convbf", [P, NIC], F32)
    lng_sb = sb("lng_sb", [P, NDC], F32)
    lnb_sb = sb("lnb_sb", [P, NDC], F32)
    gbf = sb("gbf", [1, 8], F32)
    gbb = sb("gbb", [1, 8], BF16)
    onehot = sb("onehot_sb", [NEXP, NEXP * P], BF16)
    geF = sb("geF", [NEXP, C], BF16)
    gebc = sb("gebc", [P, 2 * C], BF16)
    probs = sb("probs", [P, 32], F32)
    ge_sb = sb("ge_sb", [P, 32], BF16)
    eqs = sb("eqs", [P, 32], F32)
    tts = sb("tts", [P, 32], F32)
    msk1 = sb("msk1", [P, 32], F32)
    msk2 = sb("msk2", [P, 32], F32)
    p2s = sb("p2s", [P, 32], F32)
    m1c = sb("m1c", [P, 4], F32)
    m2c = sb("m2c", [P, 4], F32)
    r1c = sb("r1c", [P, 4], F32)
    r2c = sb("r2c", [P, 4], F32)
    dinvc = sb("dinvc", [P, 4], F32)
    w1c = sb("w1c", [P, 4], F32)
    w2c = sb("w2c", [P, 4], F32)
    row_mean = sb("row_mean", [1, C], F32)
    row_a = sb("row_a", [1, C], F32)
    row_b = sb("row_b", [1, C], F32)
    s_C = sb("s_C", [16, C], BF16)
    ones_col_b = sb("ones_col_b", [P, 1], BF16)
    ones_row_b = sb("ones_row_b", [1, P], BF16)
    ones_row_f = sb("ones_row_f", [1, P], F32)
    eps_c = sb("eps_c", [1, 1], F32)
    ab_sb = sb("ab_sb", [P, 2 * C], BF16)

    pall = es.enter_context(nc.psum_tensor("pall", [P, 8 * C], F32))[:]

    def bank(i, p0=0, pn=P, c0=0, w=C):
        return pall[p0:p0 + pn, i * C + c0:i * C + c0 + w]

    def siluz(mc):
        return uni[:, mc * C:(mc + 1) * C]

    def xconv(mc):
        return uni[:, (NIC + mc) * C:(NIC + mc + 1) * C]

    def yattn(mc):
        return uni[:, (2 * NIC + mc) * C:(2 * NIC + mc + 1) * C]

    def hall(kc):
        return uni[:, kc * C:(kc + 1) * C]

    def xc(t, i):
        return t[:, i * C:(i + 1) * C]

    em = EM(nc, es)
    for s in ("t", "v", "s", "g", "sl0", "sl1", "sl2", "sl3", "xin", "xout",
              "cons", "dbg", "cast"):
        em.sem(s)

    T, V, S, SY, G = "tensor", "vector", "scalar", "sync", "gpsimd"

    def ld(dst, src_name, nk=None, nj=None):
        src = dr[src_name]
        if nk is not None:
            src = src.rearrange("(k p) j -> p k j", p=P)
            dst = dst.rearrange("p (k j) -> p k j", k=nk)
        return em.op(SY, lambda E, d=dst, s=src: E.dma_start(d, s), inc="cons", inc_by=16)

    _src30 = x_hbm.rearrange("(k p) t -> p k t", p=P)[:, :, 0:C]
    _dst30 = x2_sb[:, 0:NDC * C].rearrange("p (k t) -> p k t", k=NDC)
    ev_xload0 = em.op(SY, lambda E, d=_dst30, s=_src30: E.dma_start(d, s),
                      inc="xin", inc_by=16)

    ld(identB, "identB")
    ld(riota, "riota")
    ld(gwTf, "gwT", NDC, 8)
    ld(wxcf, "wxcT", NIC, 16)
    ld(x2_sb[0:16, NDC * C:NDC * C + DI], "blogT")
    ld(convwf, "convw")
    ld(convbf, "convb")
    ld(lng_sb, "lng")
    ld(lnb_sb, "lnb")
    ld(gbf, "gb")
    ld(onehot, "onehot")
    ev_cons = em.ev("cons")

    em.op(V, lambda E: E.memset(dec, 0.95))
    em.op(V, lambda E: E.memset(probs, -2.0))
    em.op(V, lambda E: E.memset(hist, 0.0))
    em.op(V, lambda E: E.memset(ones_col_b, 1.0))
    em.op(V, lambda E: E.memset(ones_row_b, 1.0))
    em.op(V, lambda E: E.memset(ones_row_f, 1.0))
    em.op(V, lambda E: E.memset(eps_c, float(EPS)))

    em.op(V, lambda E: E.memset(xin2, 0.0))
    em.op(V, lambda E: E.memset(hinit, 0.0))
    em.op(V, lambda E: E.tensor_copy(gwTb, gwTf), waits=[ev_cons])
    em.op(V, lambda E: E.tensor_copy(wxcb, wxcf))
    ev_vprep = em.op(V, lambda E: E.tensor_copy(gbb, gbf), inc="v")
    ev_bexp = em.op(S, lambda E: E.activation(B_T, x2_sb[0:16, NDC * C:NDC * C + DI],
                                              AF.Exp),
                    waits=[ev_cons], inc="s")

    set_srcs_cache = {}
    slot_cnt = [0, 0, 0, 0]
    ev_setdone = {}
    ev_setload = {}
    pending_g = [0]

    def set_srcs(s):
        if s >= 14:
            s -= 4
        if s == 0:
            return [("winTa", None, 0, 16384)]
        if s == 1:
            return [("winTb", None, 0, 16384)]
        if s == 2:
            return [("woutT", None, 0, 16384)]
        if 3 <= s <= 8:
            e = s - 3
            return [("ew1T", e, 0, 8192), ("ew3T", e, 8192, 8192)]
        if s == 9:
            return [("fc1T", None, 0, 8192), ("fc2T", None, 8192, 8192)]
        if 10 <= s <= 12:
            e0 = (s - 10) * 2
            return [("ew2T", e0, 0, 8192), ("ew2T", e0 + 1, 8192, 8192)]
        return [("fc3T", None, 0, 8192)]

    def set_off(s):
        if s >= 14:
            s -= 4
        return s * 16384 if s <= 13 else None

    def set_width(s):
        if s >= 14:
            s -= 4
        return 8192 if s == 13 else 16384

    ev_wb = {}
    ev_wb_all = [None]

    wb_next = [0]

    def issue_writeback(g):
        if not BF_CACHE or g < 2:
            return
        while wb_next[0] <= min(g - 1, 13):
            s2 = wb_next[0]
            wb_next[0] += 1
            si2 = s2 % 2
            wid = set_width(s2)
            off = set_off(s2)
            em.op(SY, lambda E, d=dr["wslab"][:, off:off + wid], sc=slot[si2][:, 0:wid]:
                  E.dma_start(d, sc), waits=[ev_setload[s2]], inc="cast", inc_by=16)
            ev_wb[s2] = em.ev("cast")
            if s2 == 13:
                ev_wb_all[0] = em.ev("cast")

    def issue_set(g):
        s = g % NSETS
        si = g % 2
        bf = BF_CACHE and g >= NSETS
        issue_writeback(g)
        if s >= 14 and g < 2 * NSETS:
            return ev_setload[g - 4]
        w = [ev_setdone[g - 2]] if g - 2 >= 0 else []
        if g - 2 in ev_wb:
            w.append(ev_wb[g - 2])
        if bf:
            s_reg = s - 4 if s >= 14 else s
            w.append(ev_wb[s_reg])
            off = set_off(s)
            wid = set_width(s)
            slot_cnt[si] += 16
            em.op(G, lambda E, d=slot[si][:, 0:wid], sc=dr["wslab"][:, off:off + wid]:
                  E.dma_start(d, sc), waits=w, inc=f"sl{si}", inc_by=16)
            return (f"sl{si}", slot_cnt[si])
        else:
            for j, (nm, e, dst0, width) in enumerate(set_srcs(s)):
                src = dr[nm]
                if e is not None:
                    src = src[e * D:(e + 1) * D, :]
                src = src.rearrange("(k p) j -> p k j", p=P)
                nk = src.shape[1]
                dst = slot[si][:, dst0:dst0 + width].rearrange("p (k j) -> p k j", k=nk)
                slot_cnt[si] += 16
                em.op(G, lambda E, d=dst, sc=src: E.dma_start(d, sc),
                      waits=w if j == 0 else (), inc=f"sl{si}", inc_by=16)
        return (f"sl{si}", slot_cnt[si])

    def ensure_sets(upto_g):
        upto_g = min(upto_g, NMACRO * NSETS - 1)
        while pending_g[0] <= upto_g:
            ev_setload[pending_g[0]] = issue_set(pending_g[0])
            pending_g[0] += 1

    def mark_setdone(g, ev):
        ev_setdone[g] = ev
        ensure_sets(g + 2)

    ensure_sets(1)

    ev_store = None
    ev_resid2 = [None] * NDC
    ev_xh_te_free = None
    ev_xbf_lastuse = None
    ev_bankfree = {}
    ev_hall = {}

    def ln_stats(src_bf, ev_src_per_chunk, sb, free_evs=()):
        ev_sq = [None] * NDC
        ev_stats_list = [None] * NDC
        te_entry = em.ev("t")
        for o in range(NDC):
            wsq = [ev_src_per_chunk[o],
                   te_entry if o < 2 else ev_stats_list[o - 2]]
            ev_sq[o] = em.op(S, lambda E, o=o: E.activation(
                xc(r2rot, o % 2), xc(src_bf, o), AF.Square), waits=wsq, inc="s")
            w = [ev_src_per_chunk[o], ev_vprep, ev_cons]
            if o == 0:
                w += list(free_evs)
            em.op(T, lambda E, o=o, sb=sb: E.matmul(
                bank(sb, 0, 1), ones_col_b, xc(src_bf, o),
                start=(o == 0), stop=(o == NDC - 1), skip_group_check=True), waits=w)
            ev_stats_list[o] = em.op(T, lambda E, o=o, sb=sb: E.matmul(
                bank(sb, 32, 1), ones_col_b, xc(r2rot, o % 2),
                start=(o == 0), stop=(o == NDC - 1), skip_group_check=True),
                waits=[ev_sq[o]], inc="t")
        ev_st = ev_stats_list[NDC - 1]
        e1 = em.op(V, lambda E, sb=sb: E.tensor_scalar(
            row_mean, bank(sb, 0, 1), 1.0 / D, None, OP.mult), waits=[ev_st], inc="v")
        e3 = em.op(V, lambda E: E.tensor_tensor(row_b, row_mean, row_mean, OP.mult),
                   waits=[e1], inc="v")
        e4 = em.op(V, lambda E, sb=sb: E.scalar_tensor_tensor(
            row_a, bank(sb, 32, 1), 1.0 / D, row_b, OP.mult, OP.subtract),
            waits=[e3], inc="v")
        e5 = em.op(S, lambda E: E.activation(row_b, row_a, AF.Sqrt, bias=eps_c),
                   waits=[e4], inc="s")
        e6 = em.op(V, lambda E: E.reciprocal(row_b, row_b), waits=[e5], inc="v")
        return (e1, e6)

    def ln_ab(rows_evs, ab, bb, mslot, extra_free=()):
        e1, e6 = rows_evs
        em.op(T, lambda E, bb=bb: E.matmul(bank(bb), ones_row_f, row_mean, start=True,
                                           stop=True, skip_group_check=True),
              waits=[e1] + list(extra_free))
        ev_t = em.op(T, lambda E, ab=ab: E.matmul(bank(ab), ones_row_f, row_b,
                                                  start=True, stop=True,
                                                  skip_group_check=True),
                     waits=[e6], inc="t")
        mt = ab_sb[:, 0:C]
        rt = ab_sb[:, C:2 * C]
        em.op(S, lambda E, mt=mt, bb=bb: E.copy(mt, bank(bb)), waits=[ev_t])
        ev_s = em.op(S, lambda E, rt=rt, ab=ab: E.copy(rt, bank(ab)), inc="s")
        return (ev_s, mt, rt)

    def ln_stats_and_ab(src_bf, ev_src_per_chunk, extra_b67_free=()):
        rows = ln_stats(src_bf, ev_src_per_chunk, 6,
                        free_evs=[ev_bankfree.get(6), ev_bankfree.get(7)]
                        + list(extra_b67_free))
        return ln_ab(rows, 6, 7)

    ev_store_par = [None, None]

    def emit_xload_casts_stats1(m, free_evs):
        par_m = m % 2
        xv = x2_sb[:, par_m * NDC * C:(par_m + 1) * NDC * C]
        xvc = [xv[:, i * C:(i + 1) * C] for i in range(NDC)]
        if m == 0:
            ev_xload = ev_xload0
        else:
            w = [ev_store_par[par_m]] if ev_store_par[par_m] is not None else []
            if m == 1:
                w.append(ev_bexp)
            src3 = x_hbm.rearrange("(k p) t -> p k t", p=P)[:, :, m * C:(m + 1) * C]
            dst3 = xv.rearrange("p (k t) -> p k t", k=NDC)
            ev_xload = em.op(SY, lambda E, d=dst3, s=src3: E.dma_start(d, s),
                             waits=w, inc="xin", inc_by=16)
        ev_cast1 = [None] * NDC
        for o in range(NDC):
            w = [ev_xload]
            if ev_xbf_lastuse is not None:
                w.append(ev_xbf_lastuse)
            ev_cast1[o] = em.op(S, lambda E, o=o, a=xvc[o]: E.copy(xc(xbf, o), a),
                                waits=w, inc="s")
        rows = ln_stats(xbf, ev_cast1, 2, free_evs=free_evs)
        return rows

    ln1_rows = None
    for m in range(NMACRO):
        g0 = m * NSETS
        seq_start = (m % SEQ_MACROS == 0)
        par_m = m % 2
        xv = x2_sb[:, par_m * NDC * C:(par_m + 1) * NDC * C]
        xvc = [xv[:, i * C:(i + 1) * C] for i in range(NDC)]
        src3 = x_hbm.rearrange("(k p) t -> p k t", p=P)[:, :, m * C:(m + 1) * C]
        dst3 = xv.rearrange("p (k t) -> p k t", k=NDC)

        if m == 0:
            ln1_rows = emit_xload_casts_stats1(0, [])

        ev_ab1, mt1, rt1 = ln_ab(ln1_rows, 2, 3, 0)
        w_xh_free = [ev_xh_te_free] if ev_xh_te_free is not None else []
        ev_sub1 = [None] * NDC
        ev_xh1 = [None] * NDC

        def ln1_sub(o):
            ev_sub1[o] = em.op(V, lambda E, o=o, mt=mt1: E.tensor_tensor(
                xc(xh, o), xc(xbf, o), mt, OP.subtract),
                waits=([ev_ab1] + w_xh_free) if o == 0 else (), inc="v")

        def ln1_mul(o):
            ev_xh1[o] = em.op(V, lambda E, o=o, rt=rt1: E.tensor_tensor(
                xc(xh, o), xc(xh, o), rt, OP.mult),
                waits=[ev_sub1[o]], inc="v")

        for o in range(NDC):
            ln1_sub(o)
            if o >= 1:
                ln1_mul(o - 1)
        ln1_mul(NDC - 1)
        ev_xhat_all = ev_xh1[NDC - 1]

        ev_sileva = [None] * NIC
        ev_zev = [None] * NIC
        ev_xinev = [None] * NIC
        ev_convdone = [None] * NIC
        for half in range(2):
            g = g0 + half
            for mc in range(16):
                pb = mc % 4
                w = [ev_setload[g]]
                if half == 0:
                    w.append(ev_xinev[mc - 4] if mc >= 4 else ev_bankfree.get(pb))
                else:
                    w.append(ev_zev[mc - 4] if mc >= 4 else ev_xinev[12 + mc])
                ev_mm = None
                for k in range(NDC):
                    lhs = slot[g % 2][:, k * 2048 + mc * P:k * 2048 + (mc + 1) * P]
                    ev_mm = em.op(T, lambda E, lhs=lhs, k=k, pb=pb: E.matmul(
                        bank(pb), lhs, xc(xh, k),
                        start=(k == 0), stop=(k == NDC - 1), skip_group_check=True),
                        waits=(w + [ev_xh1[k]]) if k == 0 else [ev_xh1[k]],
                        inc="t" if k == NDC - 1 else None)
                if half == 0:
                    par = mc % 2
                    wh = [ev_convdone[mc - 2]] if mc >= 2 else [ev_vprep]
                    if seq_start:
                        ev_hp = em.op(V, lambda E, par=par: E.memset(
                            xin2[:, par * 515:par * 515 + 3], 0.0), waits=wh, inc="v")
                    else:
                        ev_hp = em.op(V, lambda E, mc=mc, par=par: E.tensor_copy(
                            xin2[:, par * 515:par * 515 + 3],
                            hist[:, mc * 3:mc * 3 + 3]), waits=wh, inc="v")
                    ev_xinev[mc] = em.op(S, lambda E, par=par, pb=pb: E.copy(
                        xin2[:, par * 515 + 3:par * 515 + 515], bank(pb)),
                        waits=[ev_mm, ev_hp], inc="s")
                    em.op(V, lambda E, mc=mc, par=par: E.tensor_copy(
                        hist[:, mc * 3:mc * 3 + 3],
                        xin2[:, par * 515 + 512:par * 515 + 515]),
                        waits=[ev_xinev[mc]])
                    if mc % 2 == 1:
                        evs = {}
                        for tap in range(4):
                            for mm in (mc - 1, mc):
                                pp = mm % 2
                                rhs = xin2[:, pp * 515 + tap:pp * 515 + tap + 512]
                                cdst = cvt[:, pp * C:(pp + 1) * C]
                                if tap == 0:
                                    wcv = [ev_xinev[mm], ev_vprep]
                                    if mm >= 2:
                                        wcv.append(ev_sileva[mm - 2])
                                    evs[mm] = em.op(V, lambda E, rhs=rhs, cdst=cdst,
                                                    mm=mm: E.tensor_scalar(
                                        cdst, rhs, convwf[:, mm * 4:mm * 4 + 1],
                                        None, OP.mult), waits=wcv, inc="v")
                                else:
                                    evs[mm] = em.op(V, lambda E, rhs=rhs, cdst=cdst,
                                                    mm=mm, tap=tap:
                                                    E.scalar_tensor_tensor(
                                        cdst, rhs,
                                        convwf[:, mm * 4 + tap:mm * 4 + tap + 1],
                                        cdst, OP.mult, OP.add),
                                        waits=[evs[mm]], inc="v")
                        for mm in (mc - 1, mc):
                            ppx = mm % 2
                            ev_convdone[mm] = evs[mm]
                            ev_sileva[mm] = em.op(S, lambda E, mm=mm,
                                                  cdst=cvt[:, (mm % 2) * C:
                                                           ((mm % 2) + 1) * C]:
                                                  E.activation(
                                xconv(mm), cdst, AF.Silu,
                                bias=convbf[:, mm:mm + 1]), waits=[evs[mm]], inc="s")
                else:
                    ev_zev[mc] = em.op(S, lambda E, mc=mc, pb=pb: E.activation(
                        siluz(mc), bank(pb), AF.Silu), waits=[ev_mm], inc="s")
            mark_setdone(g, em.ev("t"))

        if TRUNC < 2:
            ev_resid2 = [None] * NDC
            ev_store = em.op(SY, lambda E, m=m, s=dst3: E.dma_start(
                out_hbm.rearrange("(k p) t -> p k t", p=P)[:, :, m * C:(m + 1) * C], s),
                waits=[em.ev("v"), em.ev("s"), em.ev("t")], inc="xout", inc_by=16)
            ev_store_par[par_m] = ev_store
            ev_xh_te_free = em.ev("t")
            ev_xbf_lastuse = em.ev("t")
            for o in range(NDC):
                ev_resid2[o] = em.ev("v")
            ev_bankfree = {}
            for gg in range(g0, g0 + NSETS):
                if gg not in ev_setdone:
                    mark_setdone(gg, em.ev("t"))
            continue

        ev_wx = None
        for k in range(NIC):
            w = [ev_sileva[k], ev_vprep]
            if k == 0:
                w.append(ev_bankfree.get(4))
            ev_wx = em.op(T, lambda E, k=k: E.matmul(
                bank(4, 0, 16), wxcb[:, k * 16:(k + 1) * 16], xconv(k),
                start=(k == 0), stop=(k == NIC - 1), skip_group_check=True),
                waits=w, inc="t" if k == NIC - 1 else None)
        ev_sc = em.op(V, lambda E: E.tensor_copy(s_C, bank(4, 0, 16)),
                      waits=[ev_wx, ev_bexp], inc="v")

        g = g0 + 2
        ev_cg = [None] * NIC
        ev_scan = [None] * NIC
        ev_g2 = [None] * NIC
        ev_y = [None] * NIC
        ev_wo_stop = [None] * NDC
        ev_resid1 = [None] * NDC

        def emit_cg(mc):
            cgb = 6 + mc % 2
            w = [ev_sc, ev_xhat_all]
            if mc >= 2:
                w.append(ev_g2[mc - 2])
            else:
                w.append(ev_bankfree.get(cgb))
            ev_cg[mc] = em.op(T, lambda E, mc=mc, cgb=cgb: E.matmul(
                bank(cgb), B_T[:, mc * P:(mc + 1) * P], s_C,
                start=True, stop=True, skip_group_check=True), waits=w, inc="t")

        def emit_wout_col(c):
            for o in range(6):
                lhs = slot[g % 2][:, c * 1024 + o * P:c * 1024 + (o + 1) * P]
                w = []
                if o == 0:
                    w = [ev_y[c], ev_setload[g]]
                    if c == 0:
                        w += [ev_zev[12], ev_zev[13], ev_zev[14], ev_zev[15],
                              ev_sc, ev_bankfree.get(5)]
                ev = em.op(T, lambda E, lhs=lhs, c=c, o=o: E.matmul(
                    bank(o), lhs, yattn(c), start=(c == 0), stop=(c == NIC - 1),
                    skip_group_check=True), waits=w,
                    inc="t" if c == NIC - 1 else None)
                if c == NIC - 1:
                    ev_wo_stop[o] = ev

        for mc in range(NIC):
            emit_cg(mc)
            par = mc % 2
            w = [ev_sileva[mc]]
            if mc >= 2:
                w.append(ev_y[mc - 2])
            init = 0.0 if seq_start else hinit[:, mc:mc + 1]
            ev_scan[mc] = em.op(V, lambda E, mc=mc, par=par, init=init:
                                E.tensor_tensor_scan(
                xc(h2, par), dec, xconv(mc), init, OP.mult, OP.add), waits=w, inc="v")
            ev_g2[mc] = em.op(V, lambda E, mc=mc, par=par: E.tensor_tensor(
                xc(g2b, par), bank(6 + mc % 2), siluz(mc), OP.mult),
                waits=[ev_cg[mc], ev_zev[mc]], inc="v")
            ev_y[mc] = em.op(G, lambda E, mc=mc, par=par: E.tensor_tensor(
                yattn(mc), xc(h2, par), xc(g2b, par), OP.mult),
                waits=[ev_scan[mc], ev_g2[mc]], inc="g")
            em.op(V, lambda E, mc=mc, par=par: E.tensor_copy(
                hinit[:, mc:mc + 1], h2[:, par * C + C - 1:par * C + C]),
                waits=[ev_scan[mc]])
            if mc >= 2:
                emit_wout_col(mc - 2)
        emit_wout_col(NIC - 2)
        emit_wout_col(NIC - 1)
        ev_y_all = em.ev("v")
        for o in (6, 7):
            for c in range(NIC):
                lhs = slot[g % 2][:, c * 1024 + o * P:c * 1024 + (o + 1) * P]
                w = []
                if c == 0:
                    w = [ev_y_all, ev_g2[14], ev_g2[15]]
                ev = em.op(T, lambda E, lhs=lhs, c=c, o=o: E.matmul(
                    bank(o), lhs, yattn(c), start=(c == 0), stop=(c == NIC - 1),
                    skip_group_check=True), waits=w,
                    inc="t" if c == NIC - 1 else None)
                if c == NIC - 1:
                    ev_wo_stop[o] = ev
        for o in range(NDC):
            ev_resid1[o] = em.op(V, lambda E, o=o: E.tensor_tensor(
                xc(xh, o), xc(xh, o), bank(o), OP.add),
                waits=[ev_wo_stop[o]], inc="v")
        mark_setdone(g, em.ev("t"))

        if DEBUG_TAPS:
            for mc in range(NIC):
                em.op(G, lambda E, mc=mc, m=m: E.dma_start(
                    dbg_y[mc * P:(mc + 1) * P, m * C:(m + 1) * C], yattn(mc)),
                    waits=[ev_y[mc]], inc="dbg", inc_by=16)


        if DEBUG_TAPS:
            for o in range(NDC):
                em.op(G, lambda E, o=o, m=m: E.dma_start(
                    dbg_r[o * P:(o + 1) * P, m * C:(m + 1) * C], xc(xh, o)),
                    waits=[ev_resid1[o]], inc="dbg", inc_by=16)

        rows2 = ln_stats(xh, ev_resid1, 6,
                         free_evs=[ev_bankfree.get(6), ev_bankfree.get(7),
                                   ev_resid1[6], ev_resid1[7]])
        ev_ab2, mt2, rt2 = ln_ab(rows2, 6, 7, 1)
        ev_t1 = [None] * NDC
        ev_t2 = [None] * NDC
        ev_t3 = [None] * NDC
        ev_resid_out = [None] * NDC
        ev_cast2 = [None] * NDC

        def inner_t1(o):
            ev_t1[o] = em.op(V, lambda E, o=o, mt=mt2: E.tensor_tensor(
                xc(xbf, o), xc(xh, o), mt, OP.subtract),
                waits=[ev_ab2] if o == 0 else [ev_t1[o - 1]], inc="v")

        def inner_t2(o):
            ev_t2[o] = em.op(V, lambda E, o=o, rt=rt2: E.tensor_tensor(
                xc(xbf, o), xc(xbf, o), rt, OP.mult), waits=[ev_t1[o]], inc="v")

        def inner_t3(o):
            ev_t3[o] = em.op(S, lambda E, o=o: E.activation(
                xc(xbf, o), xc(xbf, o), AF.Identity,
                bias=lnb_sb[:, o:o + 1], scale=lng_sb[:, o:o + 1]),
                waits=[ev_t2[o]], inc="s")

        def inner_resid(o):
            ev_resid_out[o] = em.op(G, lambda E, o=o, a=xvc[o]: E.tensor_tensor(
                a, a, xc(xbf, o), OP.add), waits=[ev_t3[o]], inc="g")
            ev_cast2[o] = em.op(S, lambda E, o=o, a=xvc[o]: E.copy(xc(xbf, o), a),
                                waits=[ev_resid_out[o]], inc="s")

        for o in range(NDC):
            inner_t1(o)
            if o >= 1:
                inner_t2(o - 1)
            if o >= 2:
                inner_t3(o - 2)
            if o >= 3:
                inner_resid(o - 3)
        inner_t2(NDC - 1)
        inner_t3(NDC - 2)
        inner_t3(NDC - 1)
        for o in range(NDC - 3, NDC):
            inner_resid(o)

        rows3 = ln_stats(xbf, ev_cast2, 6, free_evs=[ev_ab2])
        ev_ab3, mt3, rt3 = ln_ab(rows3, 6, 7, 2)
        ev_xbf_lastuse = em.ev("t")
        ev_st2pc = [None] * NDC
        ev_xh2pc = [None] * NDC

        def ln2_sub(o):
            ev_st2pc[o] = em.op(V, lambda E, o=o, mt=mt3: E.tensor_tensor(
                xc(xh, o), xc(xbf, o), mt, OP.subtract),
                waits=[ev_ab3] if o == 0 else (), inc="v")

        def ln2_mul(o):
            ev_xh2pc[o] = em.op(V, lambda E, o=o, rt=rt3: E.tensor_tensor(
                xc(xh, o), xc(xh, o), rt, OP.mult),
                waits=[ev_st2pc[o]], inc="v")

        for o in range(NDC):
            ln2_sub(o)
            if o >= 1:
                ln2_mul(o - 1)
        ln2_mul(NDC - 1)
        ev_xh2_all = ev_xh2pc[NDC - 1]

        ev_lg = [None] * 4
        for tc in range(4):
            w = [ev_vprep]
            for k in range(NDC):
                lhs = xh[:, k * C + tc * P:k * C + (tc + 1) * P]
                em.op(T, lambda E, lhs=lhs, tc=tc, k=k: E.matmul(
                    bank(6, 0, P, tc * 8, 6), lhs, gwTb[:, k * 8:k * 8 + 6],
                    start=(k == 0), stop=False, skip_group_check=True),
                    waits=(w + [ev_xh2pc[k]]) if k == 0 else [ev_xh2pc[k]])
            ev_lg[tc] = em.op(T, lambda E, tc=tc: E.matmul(
                bank(6, 0, P, tc * 8, 6), ones_row_b, gbb[0:1, 0:6],
                start=False, stop=True, skip_group_check=True), inc="t")
        ev_sig = [None] * 4
        for tc in range(4):
            ev_sig[tc] = em.op(S, lambda E, tc=tc: E.activation(
                probs[:, tc * 8:tc * 8 + 6], bank(6, 0, P, tc * 8, 6), AF.Sigmoid),
                waits=[ev_lg[3]], inc="s")

        if TRUNC == 22:
            ev_resid2 = [None] * NDC
            ev_store = em.op(SY, lambda E, m=m, s=dst3: E.dma_start(
                out_hbm.rearrange("(k p) t -> p k t", p=P)[:, :, m * C:(m + 1) * C], s),
                waits=[em.ev("v"), em.ev("s"), em.ev("t")], inc="xout", inc_by=16)
            ev_store_par[par_m] = ev_store
            ev_xh_te_free = em.ev("t")
            ev_xbf_lastuse = em.ev("t")
            for o in range(NDC):
                ev_resid2[o] = em.ev("v")
            ev_bankfree = {}
            for gg in range(g0, g0 + NSETS):
                if gg not in ev_setdone:
                    mark_setdone(gg, em.ev("t"))
            continue

        probs3 = probs.rearrange("p (a b) -> p a b", a=4)
        p2s3 = p2s.rearrange("p (a b) -> p a b", a=4)
        eqs3 = eqs.rearrange("p (a b) -> p a b", a=4)
        tts3 = tts.rearrange("p (a b) -> p a b", a=4)
        msk13 = msk1.rearrange("p (a b) -> p a b", a=4)
        msk23 = msk2.rearrange("p (a b) -> p a b", a=4)

        def bc4(t):
            return bass.AP(t.tensor, t.offset, [[4, P], [1, 4], [0, 8]])

        riota_b = bass.AP(riota.tensor, riota.offset, [[8, P], [0, 4], [1, 8]])
        ge3 = ge_sb.rearrange("p (a b) -> p a b", a=4)
        ev = em.op(V, lambda E: E.reduce_max(m1c, probs3, axis=AX.X),
                   waits=[ev_sig[0], ev_sig[1], ev_sig[2], ev_sig[3]], inc="v")
        ev = em.op(V, lambda E: E.tensor_tensor(eqs3, probs3, bc4(m1c), OP.is_equal),
                   waits=[ev], inc="v")
        ev = em.op(V, lambda E: E.tensor_tensor(tts3, eqs3, riota_b, OP.mult),
                   waits=[ev], inc="v")
        ev = em.op(V, lambda E: E.reduce_max(r1c, tts3, axis=AX.X), waits=[ev], inc="v")
        ev = em.op(V, lambda E: E.tensor_tensor(msk13, tts3, bc4(r1c), OP.is_equal),
                   waits=[ev], inc="v")
        ev = em.op(V, lambda E: E.scalar_tensor_tensor(p2s, msk1, -1e30, probs,
                                                       OP.mult, OP.add),
                   waits=[ev], inc="v")
        ev = em.op(V, lambda E: E.reduce_max(m2c, p2s3, axis=AX.X), waits=[ev], inc="v")
        ev = em.op(V, lambda E: E.tensor_tensor(eqs3, p2s3, bc4(m2c), OP.is_equal),
                   waits=[ev], inc="v")
        ev = em.op(V, lambda E: E.tensor_tensor(tts3, eqs3, riota_b, OP.mult),
                   waits=[ev], inc="v")
        ev = em.op(V, lambda E: E.reduce_max(r2c, tts3, axis=AX.X), waits=[ev], inc="v")
        ev_mk2 = em.op(V, lambda E: E.tensor_tensor(msk23, tts3, bc4(r2c), OP.is_equal),
                       waits=[ev], inc="v")
        ev = em.op(V, lambda E: E.scalar_tensor_tensor(dinvc, m1c, 1e-8, m2c,
                                                       OP.add, OP.add),
                   waits=[ev], inc="v")
        ev = em.op(V, lambda E: E.reciprocal(dinvc, dinvc), waits=[ev], inc="v")
        ev_w1 = em.op(V, lambda E: E.tensor_tensor(w1c, m1c, dinvc, OP.mult),
                      waits=[ev], inc="v")
        ev_w2 = em.op(V, lambda E: E.tensor_tensor(w2c, m2c, dinvc, OP.mult),
                      waits=[ev_w1], inc="v")
        ev = em.op(V, lambda E: E.tensor_tensor(eqs3, msk13, bc4(w1c), OP.mult),
                   waits=[ev_w1], inc="v")
        ev = em.op(V, lambda E: E.tensor_tensor(tts3, msk23, bc4(w2c), OP.mult),
                   waits=[ev_w2, ev_mk2], inc="v")
        ev_ge_all = em.op(V, lambda E: E.tensor_tensor(ge_sb, eqs, tts, OP.add),
                          waits=[ev], inc="v")

        geF_state = {}

        def emit_geT():
            ev_tr = None
            for tc in range(4):
                ev_tr = em.op(T, lambda E, tc=tc: E.matmul(
                    bank(7, 0, 6, tc * P, P), ge_sb[:, tc * 8:tc * 8 + 6], identB,
                    start=True, stop=True, skip_group_check=True),
                    waits=[ev_ge_all] if tc == 0 else (), inc="t")
            for tc in range(4):
                em.op(V, lambda E, tc=tc: E.tensor_copy(
                    geF[:, tc * P:(tc + 1) * P], bank(7, 0, 6, tc * P, P)),
                    waits=[ev_tr], inc="v" if tc == 3 else None)
            geF_state["ev"] = em.ev("v")

        if TRUNC < 4:
            ev_resid2 = [None] * NDC
            ev_store = em.op(SY, lambda E, m=m, s=dst3: E.dma_start(
                out_hbm.rearrange("(k p) t -> p k t", p=P)[:, :, m * C:(m + 1) * C], s),
                waits=[em.ev("v"), em.ev("s"), em.ev("t")], inc="xout", inc_by=16)
            ev_store_par[par_m] = ev_store
            ev_xh_te_free = em.ev("t")
            ev_xbf_lastuse = em.ev("t")
            for o in range(NDC):
                ev_resid2[o] = em.ev("v")
            ev_bankfree = {}
            for gg in range(g0, g0 + NSETS):
                if gg not in ev_setdone:
                    mark_setdone(gg, em.ev("t"))
            continue

        last_h_by_bank = {0: ev_resid1[0], 1: ev_resid1[1],
                          2: ev_resid1[2], 3: ev_resid1[3]}
        last_ssil = [None, None]
        ev_gebc_done = [None, None]
        for e in range(7):
            g = g0 + (3 + e if e < 6 else 9)
            epar = e % 2
            ev_mm1_l = [None] * NDC
            ev_mm3_l = [None] * NDC

            def emit_h_mms(mc, e=e, g=g):
                b1 = mc % 2
                b3 = 2 + mc % 2
                w1 = [ev_xh2_all, ev_setload[g], last_h_by_bank[b1]]
                w3 = [last_h_by_bank[b3]]
                ev = None
                for k in range(NDC):
                    lhs = slot[g % 2][:, k * 1024 + mc * P:k * 1024 + (mc + 1) * P]
                    ev = em.op(T, lambda E, lhs=lhs, k=k, b1=b1: E.matmul(
                        bank(b1), lhs, xc(xh, k), start=(k == 0), stop=(k == NDC - 1),
                        skip_group_check=True), waits=w1 if k == 0 else (),
                        inc="t" if k == NDC - 1 else None)
                ev_mm1_l[mc] = ev
                for k in range(NDC):
                    lhs = slot[g % 2][:, 8192 + k * 1024 + mc * P:
                                      8192 + k * 1024 + (mc + 1) * P]
                    ev = em.op(T, lambda E, lhs=lhs, k=k, b3=b3: E.matmul(
                        bank(b3), lhs, xc(xh, k), start=(k == 0), stop=(k == NDC - 1),
                        skip_group_check=True), waits=w3 if k == 0 else (),
                        inc="t" if k == NDC - 1 else None)
                ev_mm3_l[mc] = ev
                last_h_by_bank[b1] = ("pending", mc)
                last_h_by_bank[b3] = ("pending", mc)

            def emit_gebc(e=e, epar=epar):
                gb_bank = 4 + epar
                w = [geF_state["ev"], ev_cons, ev_resid1[4 + epar],
                     ev_gebc_done[epar]]
                ev_bc = em.op(T, lambda E, e=e, gb_bank=gb_bank: E.matmul(
                    bank(gb_bank), onehot[:, e * P:(e + 1) * P], geF,
                    start=True, stop=True, skip_group_check=True), waits=w, inc="t")
                ev_gebc_done[epar] = em.op(V, lambda E, epar=epar, gb_bank=gb_bank:
                                           E.tensor_copy(xc(gebc, epar), bank(gb_bank)),
                                           waits=[ev_bc], inc="v")

            def resolve_pending():
                for b in (0, 1, 2, 3):
                    v = last_h_by_bank[b]
                    if isinstance(v, tuple) and v[0] == "pending":
                        kcp = e * 8 + v[1]
                        if kcp in ev_hall and ev_hall[kcp] is not None:
                            last_h_by_bank[b] = ev_hall[kcp]

            emit_h_mms(0)
            emit_h_mms(1)
            if e == 0:
                emit_geT()
            if e < 6:
                emit_gebc()
            for mc in range(NDC):
                kc = e * 8 + mc
                b1 = mc % 2
                b3 = 2 + mc % 2
                spar = mc % 2
                ev_sl = em.op(S, lambda E, spar=spar, b1=b1: E.activation(
                    xc(ssil, spar), bank(b1), AF.Silu),
                    waits=[ev_mm1_l[mc], last_ssil[spar]], inc="s")
                if e < 6:
                    ev_op1 = em.op(V, lambda E, kc=kc, b3=b3, epar=epar: E.tensor_tensor(
                        hall(kc), bank(b3), xc(gebc, epar), OP.mult),
                        waits=[ev_mm3_l[mc], ev_gebc_done[epar]], inc="v")
                    ev_hall[kc] = em.op(G, lambda E, kc=kc, spar=spar: E.tensor_tensor(
                        hall(kc), hall(kc), xc(ssil, spar), OP.mult),
                        waits=[ev_op1, ev_sl], inc="g")
                else:
                    ev_op1 = em.op(V, lambda E, kc=kc, spar=spar, b3=b3:
                                   E.tensor_tensor(
                        hall(kc), bank(b3), xc(ssil, spar), OP.mult),
                        waits=[ev_mm3_l[mc], ev_sl], inc="v")
                    ev_hall[kc] = ev_op1
                last_ssil[spar] = ev_hall[kc]
                if mc + 2 < NDC:
                    last_h_by_bank[(mc + 2) % 2] = ev_sl
                    last_h_by_bank[2 + (mc + 2) % 2] = ev_op1
                    emit_h_mms(mc + 2)
                else:
                    last_h_by_bank[b1] = ev_sl
                    last_h_by_bank[b3] = ev_op1
            mark_setdone(g, em.ev("t"))

        if TRUNC < 5:
            ev_resid2 = [None] * NDC
            ev_store = em.op(SY, lambda E, m=m, s=dst3: E.dma_start(
                out_hbm.rearrange("(k p) t -> p k t", p=P)[:, :, m * C:(m + 1) * C], s),
                waits=[em.ev("v"), em.ev("s"), em.ev("t")], inc="xout", inc_by=16)
            ev_store_par[par_m] = ev_store
            ev_xh_te_free = em.ev("t")
            ev_xbf_lastuse = em.ev("t")
            for o in range(NDC):
                ev_resid2[o] = em.ev("v")
            ev_bankfree = {}
            for gg in range(g0, g0 + NSETS):
                if gg not in ev_setdone:
                    mark_setdone(gg, em.ev("t"))
            continue

        ev_ph2_start_v = em.ev("v")
        ev_ph2_start_s = em.ev("s")

        def ph2_half(half):
            ev_ph2 = None
            base = g0 + (10 if half == 0 else 14)
            for kc in range(NKC):
                if kc < 48:
                    g = base + kc // 16
                    off = ((kc // 8) % 2) * 8192 + (kc % 8) * 1024
                else:
                    g = base + 3
                    off = (kc - 48) * 1024
                for o in range(half * 4, half * 4 + 4):
                    lhs = slot[g % 2][:, off + o * P:off + (o + 1) * P]
                    w = []
                    if o == half * 4:
                        w = [ev_hall[kc], ev_setload[g]]
                        if kc == 0:
                            w += [ev_ph2_start_v, ev_ph2_start_s]
                    ev_ph2 = em.op(T, lambda E, lhs=lhs, kc=kc, o=o: E.matmul(
                        bank(o), lhs, hall(kc), start=(kc == 0), stop=(kc == NKC - 1),
                        skip_group_check=True), waits=w,
                        inc="t" if kc == NKC - 1 else None)
                if kc in (15, 31, 47, 55):
                    mark_setdone(g, em.ev("t"))
            return ev_ph2

        ev_ph2a = ph2_half(0)
        for o in range(4):
            ev_resid2[o] = em.op(V, lambda E, o=o, a=xvc[o]: E.tensor_tensor(
                a, a, bank(o), OP.add), waits=[ev_ph2a], inc="v")
            ev_bankfree[o] = ev_resid2[o]
        if m + 1 < NMACRO:
            ln1_rows = emit_xload_casts_stats1(
                m + 1, [ev_resid2[2], ev_resid2[3]])
        ev_ph2b = ph2_half(1)
        for o in range(4, NDC):
            ev_resid2[o] = em.op(V, lambda E, o=o, a=xvc[o]: E.tensor_tensor(
                a, a, bank(o), OP.add), waits=[ev_ph2b], inc="v")
            ev_bankfree[o] = ev_resid2[o]
        dsto = out_hbm.rearrange("(k p) t -> p k t", p=P)[:, :, m * C:(m + 1) * C]
        ev_store = em.op(SY, lambda E, d=dsto, s=dst3: E.dma_start(d, s),
            waits=[ev_resid2[o] for o in range(NDC)], inc="xout", inc_by=16)
        ev_store_par[par_m] = ev_store

        ev_xh_te_free = em.ev("t")

    em.wait(SY, em.ev("xout"))

    with nc.Block() as block:

        @block.tensor
        def _(E):
            for f in em.ops["tensor"]:
                f(E)

        @block.vector
        def _(E):
            for f in em.ops["vector"]:
                f(E)

        @block.scalar
        def _(E):
            for f in em.ops["scalar"]:
                f(E)

        @block.sync
        def _(E):
            for f in em.ops["sync"]:
                f(E)

        @block.gpsimd
        def _(E):
            for f in em.ops["gpsimd"]:
                f(E)

    es.close()
    return nc


def _onehot_const():
    oh = np.zeros((NEXP, NEXP * P), dtype=ml_dtypes.bfloat16)
    for e in range(NEXP):
        oh[e, e * P:(e + 1) * P] = 1.0
    return oh


def kernel(**inputs):
    f32 = np.float32
    xin = np.asarray(inputs["L0_attn_x"], f32)
    W_in = np.asarray(inputs["L0_attn_W_in"], f32)
    conv_w = np.asarray(inputs["L0_attn_conv_w"], f32)
    conv_b = np.asarray(inputs["L0_attn_conv_b"], f32)
    W_x = np.asarray(inputs["L0_attn_W_x"], f32)
    B_log = np.asarray(inputs["L0_attn_B_log"], f32)
    W_out = np.asarray(inputs["L0_attn_W_out"], f32)
    ln_g = np.asarray(inputs["L0_attn_ln_g"], f32)
    ln_b = np.asarray(inputs["L0_attn_ln_b"], f32)
    gate_w = np.asarray(inputs["L0_ffn_gate_w"], f32)
    gate_b = np.asarray(inputs["L0_ffn_gate_b"], f32)
    ew1 = np.asarray(inputs["L0_ffn_ew1"], f32)
    ew2 = np.asarray(inputs["L0_ffn_ew2"], f32)
    ew3 = np.asarray(inputs["L0_ffn_ew3"], f32)
    fc1 = np.asarray(inputs["L0_ffn_fc1"], f32)
    fc2 = np.asarray(inputs["L0_ffn_fc2"], f32)
    fc3 = np.asarray(inputs["L0_ffn_fc3"], f32)

    winT = np.ascontiguousarray(W_in.T)
    shared = {
        "winTa": np.ascontiguousarray(winT[:, :DI]),
        "winTb": np.ascontiguousarray(winT[:, DI:]),
        "woutT": np.ascontiguousarray(W_out.T),
        "ew1T": np.ascontiguousarray(np.transpose(ew1, (0, 2, 1)).reshape(NEXP * D, D)),
        "ew3T": np.ascontiguousarray(np.transpose(ew3, (0, 2, 1)).reshape(NEXP * D, D)),
        "ew2T": np.ascontiguousarray(np.transpose(ew2, (0, 2, 1)).reshape(NEXP * D, D)),
        "fc1T": np.ascontiguousarray(fc1.T),
        "fc2T": np.ascontiguousarray(fc2.T),
        "fc3T": np.ascontiguousarray(fc3.T),
        "wxcT": np.ascontiguousarray(W_x[16:32].T),
        "blogT": np.ascontiguousarray(B_log.T),
        "gwT": np.ascontiguousarray(
            np.concatenate([gate_w.T, np.zeros((D, 2), f32)], axis=1)),
        "gb": np.concatenate([gate_b, np.zeros(2, f32)]).reshape(1, 8),
        "convw": np.ascontiguousarray(
            conv_w[:, 0, :].reshape(NIC, P, 4).transpose(1, 0, 2).reshape(P, NIC * 4)),
        "convb": np.ascontiguousarray(conv_b.reshape(NIC, P).T),
        "lng": np.ascontiguousarray(ln_g.reshape(NDC, P).T),
        "lnb": np.ascontiguousarray(ln_b.reshape(NDC, P).T),
        "identB": np.eye(P, dtype=ml_dtypes.bfloat16),
        "riota": np.broadcast_to(8.0 - np.arange(8, dtype=f32), (P, 8)).copy(),
        "onehot": _onehot_const(),
    }

    n = 8
    in_maps = []
    for c in range(n):
        xc_ = xin[2 * c:2 * c + 2].reshape(TOK, D)
        m = dict(shared)
        m["x"] = np.ascontiguousarray(xc_.T)
        in_maps.append(m)

    nc = build_nc()
    res = run_bass_kernel_spmd(nc, in_maps, core_ids=list(range(n)))
    outs = []
    for c in range(n):
        o = np.asarray(res.results[c]["out"])
        outs.append(np.ascontiguousarray(o.T).reshape(2, 4096, D))
    return np.concatenate(outs, axis=0)


if __name__ == "__main__":
    np.random.seed(0)
    import reference as _r
    ins = _r.setup_inputs(0)
    ins = {k: np.asarray(v) for k, v in ins.items()}
    out = kernel(**ins)
    ref = np.asarray(_r.reference(**ins))
    num = float(((out - ref) ** 2).mean())
    den = float((ref ** 2).mean())
    print("resid-var ratio", num / den)
